# version 16
# baseline (speedup 1.0000x reference)
import math
import numpy as np
import concourse.bass as bass
import concourse.mybir as mybir
from concourse.bass_utils import run_bass_kernel_spmd

F32 = mybir.dt.float32
BF16 = mybir.dt.bfloat16
AF = mybir.ActivationFunctionType
ALU = mybir.AluOpType
AX = mybir.AxisListType.X

S, D, NT, ND = 2048, 1024, 16, 8
H, DK, MF, LAT, HH, ER = 16, 64, 256, 256, 32, 7
LN_EPS = 1e-5
RMS_EPS = 1e-5
EPS_A = 1e-6
S1A = 1.0 / (DK ** 0.25)
S2A = 1.0 / (2.0 * math.sqrt(DK))
EPSP = EPS_A / math.sqrt(MF)
MLA_SC = 1.0 / 8.0
SQ8 = 2.0 ** -1.5

ENGS = ("sync", "tensor", "vector", "scalar", "gpsimd")


class Sem:
    __slots__ = ("h", "v", "name")

    def __init__(self, h, name):
        self.h = h
        self.v = 0
        self.name = name


class Prog:
    def __init__(self, nc):
        self.nc = nc
        self.q = {e: [] for e in ENGS}
        self._ctx = []
        self.rel = {}

    def sem(self, name):
        g = self.nc.semaphore(name)
        h = g.__enter__()
        self._ctx.append(g)
        return Sem(h, name)

    def on(self, eng, build, wait=(), inc=None, n=1):
        w = tuple((s, v) for (s, v) in wait if v > 0)
        if inc is not None:
            inc.v += n

        def f(e, build=build, w=w, inc=inc, n=n):
            for s, v in w:
                e.wait_ge(s.h, v)
            ins = build(e)
            if inc is not None:
                ins.then_inc(inc.h, n)
        self.q[eng].append(f)
        return inc.v if inc is not None else None

    def wait(self, eng, *pairs):
        ps = tuple(pairs)

        def f(e, ps=ps):
            for s, v in ps:
                if v > 0:
                    e.wait_ge(s.h, v)
        self.q[eng].append(f)

    def drain(self, eng):
        self.q[eng].append(lambda e: e.drain())

    def dma(self, eng, out, in_, sem, wait=(), transpose=False):
        def b(e, out=out, in_=in_, transpose=transpose):
            if transpose:
                return e.dma_start(out=out, in_=in_, transpose=True)
            return e.dma_start(out=out, in_=in_)
        return self.on(eng, b, wait=wait, inc=sem, n=16)

    def release(self, res, sem):
        self.rel[res] = (sem, sem.v)

    def use(self, res):
        r = self.rel.get(res)
        return [r] if r is not None else []

    def finalize(self):
        with self.nc.Block() as block:
            for name in ENGS:
                ops = self.q[name]

                def body(e, ops=ops):
                    for f_ in ops:
                        f_(e)
                getattr(block, name)(body)


def APx(t, off, dims):
    return bass.AP(t, off, [list(d) for d in dims])


def build_nc(stop_after=None):
    nc = bass.Bass("TRN2", target_bir_lowering=False)
    di = {}

    def dram_in(name, shape, dt=F32):
        di[name] = nc.dram_tensor(name, list(shape), dt, kind="ExternalInput")
        return di[name]

    x_in = dram_in("x", [S, D])
    wqa = dram_in("wqa", [D, D]); wka = dram_in("wka", [D, D])
    wva = dram_in("wva", [D, D]); woa = dram_in("woa", [D, D])
    bqa_d = dram_in("bqa", [128, ND]); bka_d = dram_in("bka", [128, ND])
    orft_d = dram_in("orfT", [128, MF])
    wqb = dram_in("wqb", [D, D]); wkb = dram_in("wkb", [D, D])
    wvb = dram_in("wvb", [D, D]); wob = dram_in("wob", [D, D])
    wkvd = dram_in("wkvd", [D, LAT]); wqd = dram_in("wqd", [D, LAT])
    wku_p = dram_in("wku_p", [LAT, 1024]); wq_c = dram_in("wq_c", [LAT, 1024])
    wvu = dram_in("wvu", [LAT, D])
    wropek = dram_in("wropek", [D, 512])
    wo_m = dram_in("wo_m", [D, D])
    w_ln1f = dram_in("w_ln1f", [128, ND]); w_ln2f = dram_in("w_ln2f", [128, ND])
    wg_s = dram_in("wg_s", [D, D]); wu_s = dram_in("wu_s", [D, D]); wd_s = dram_in("wd_s", [D, D])
    wg_r = dram_in("wg_r", [ER, D, D]); wu_r = dram_in("wu_r", [ER, D, D])
    wd_r = dram_in("wd_r", [ER, D, D])
    wrout_d = dram_in("wrout", [D, ER])
    rb_d = dram_in("rb_row", [128, ER])
    identf = dram_in("identf", [128, 128])
    identb_d = dram_in("identb", [128, 128], BF16)
    constf = dram_in("constf", [128, 16])
    constb_d = dram_in("constb", [128, 514], BF16)
    crow_d = dram_in("crow", [1, 2048], BF16)
    ropec = dram_in("ropec", [128, S], BF16)
    ropes = dram_in("ropes", [128, S], BF16)
    ropec2 = dram_in("ropec2", [128, S], BF16)
    ropes2 = dram_in("ropes2", [128, S], BF16)

    out_t = nc.dram_tensor("out", [S, D], F32, kind="ExternalOutput")
    dbg_t = None
    if stop_after is not None:
        dbg_t = nc.dram_tensor("dbg", [S, D], F32, kind="ExternalOutput")

    sp_x = nc.dram_tensor("sp_x", [S, D], F32, kind="Internal")
    d_xn = nc.dram_tensor("d_xn", [S, D], BF16, kind="Internal")
    d_ya = nc.dram_tensor("d_ya", [S, D], BF16, kind="Internal")
    d_yraw = nc.dram_tensor("d_yraw", [1040, S], BF16, kind="Internal")
    d_y2 = nc.dram_tensor("d_y2", [S, D], BF16, kind="Internal")
    d_lo = nc.dram_tensor("d_lo", [S, D], BF16, kind="Internal")

    ctx = []

    def sbt(name, shape, dt=F32):
        g = nc.sbuf_tensor(name, list(shape), dt)
        t = g.__enter__()
        ctx.append(g)
        return t

    def psb(name, shape):
        g = nc.psum_tensor(name, list(shape), F32)
        t = g.__enter__()
        ctx.append(g)
        return t

    R = sbt("R", [128, NT * D])
    XNT = sbt("XNT", [128, ND * S], BF16)
    WB0 = sbt("WB0", [128, 8 * D], BF16)
    WB1 = sbt("WB1", [128, 8 * D], BF16)
    WB2 = sbt("WB2", [128, 8 * D], BF16)
    SCA = sbt("SCA", [128, 17408], BF16)
    SCB = sbt("SCB", [128, 8192], BF16)
    SCF = sbt("SCF", [128, 2056], BF16)
    IDENT = sbt("IDENT", [128, 128], F32)
    IDENTB = sbt("IDENTB", [128, 128], BF16)
    CONST = sbt("CONST", [128, 16], F32)
    CONSTB = sbt("CONSTB", [128, 514], BF16)
    CROW = sbt("CROW", [1, 2048], BF16)
    BQA = sbt("BQA", [128, ND], F32)
    BKA = sbt("BKA", [128, ND], F32)
    WLN1 = sbt("WLN1", [128, ND], F32)
    WLN2 = sbt("WLN2", [128, ND], F32)
    RBR = sbt("RBR", [128, ER], F32)
    ORFT = sbt("ORFT", [128, MF], BF16)
    STAT = sbt("STAT", [128, 136], F32)
    SM0 = sbt("SM0", [128, 16], F32)
    WROUT = sbt("WROUT", [128, ND * ER], BF16)
    WRL = sbt("WRL", [128, ND * ER], BF16)
    RLG = sbt("RLG", [7, 128], F32)
    PR = sbt("PR", [128, 128], F32)
    WTS = STAT

    PS = [psb(f"PS{i}", [128, 512]) for i in range(8)]

    p = Prog(nc)
    s_ld = p.sem("s_ld")
    s_tp = p.sem("s_tp")
    s_st = p.sem("s_st")
    s_gld = p.sem("s_gld")
    s_mm = p.sem("s_mm")
    s_dv = p.sem("s_dv")
    s_ac = p.sem("s_ac")
    s_pp = [p.sem(f"s_pp{i}") for i in range(4)]
    s_wl = [p.sem(f"s_wl{i}") for i in range(3)]
    s_dbg = p.sem("s_dbg")
    s_rl = [p.sem(f"s_rl{i}") for i in range(4)]
    s_tb = p.sem("s_tb")
    s_sc = p.sem("s_sc")
    s_yd = p.sem("s_yd")
    s_y2 = p.sem("s_y2")

    def mm(out_ap, lhsT, rhs, start, stop, wait=(), tp=None):
        def b(e, out_ap=out_ap, lhsT=lhsT, rhs=rhs, start=start, stop=stop, tp=tp):
            return e.matmul(out_ap, lhsT, rhs, start=start, stop=stop, tile_position=tp)
        return p.on("tensor", b, wait=wait, inc=s_mm, n=1)

    def v_op(name, *a, wait=(), **kw):
        def b(e, name=name, a=a, kw=kw):
            return getattr(e, name)(*a, **kw)
        return p.on("vector", b, wait=wait, inc=s_dv, n=1)

    def a_op(name, *a, wait=(), **kw):
        def b(e, name=name, a=a, kw=kw):
            return getattr(e, name)(*a, **kw)
        return p.on("scalar", b, wait=wait, inc=s_ac, n=1)


    def big_rdma(eng, dram_t, sbuf_t, sem, wait=(), accum=False, ngrp=4):
        per = NT // ngrp
        vals = []
        for g_ in range(ngrp):
            src = APx(dram_t, g_ * per * 128 * D, [[D, 128], [128 * D, per], [1, D]])
            dst = sbuf_t[:, g_ * per * D:(g_ + 1) * per * D]
            if accum:
                def b(e, dst=dst, src=src):
                    return e.dma_start(out=dst, in_=src, accum_op=ALU.add)
                vals.append(p.on(eng, b, wait=(wait if g_ == 0 else ()), inc=sem, n=16))
            else:
                vals.append(p.dma(eng, dst, src, sem, wait=(wait if g_ == 0 else ())))
        return vals[-1]

    def big_wdma(eng, dram_t, sbuf_t, sem, wait=(), ngrp=4):
        per = NT // ngrp
        vals = []
        for g_ in range(ngrp):
            dst = APx(dram_t, g_ * per * 128 * D, [[D, 128], [128 * D, per], [1, D]])
            src = sbuf_t[:, g_ * per * D:(g_ + 1) * per * D]
            vals.append(p.dma(eng, dst, src, sem, wait=(wait if g_ == 0 else ())))
        return vals[-1]

    def ld2(dst_ap, src_t, rows, cols):
        return p.dma("sync", dst_ap, APx(src_t, 0, [[cols, rows], [1, cols]]), s_ld)

    ld2(IDENT[:, :], identf, 128, 128)
    ld2(IDENTB[:, :], identb_d, 128, 128)
    ld2(CONST[:, :], constf, 128, 16)
    ld2(CONSTB[:, :], constb_d, 128, 514)
    p.dma("sync", CROW[:, :], APx(crow_d, 0, [[2048, 1], [1, 2048]]), s_ld)
    ld2(BQA[:, :], bqa_d, 128, ND)
    ld2(BKA[:, :], bka_d, 128, ND)
    ld2(WLN1[:, :], w_ln1f, 128, ND)
    ld2(WLN2[:, :], w_ln2f, 128, ND)
    ld2(RBR[:, :], rb_d, 128, ER)
    big_rdma("sync", x_in, R, s_ld)
    v_wrt = p.dma("sync", STAT[:, 0:ND * ER],
                  APx(wrout_d, 0, [[ER, 128], [128 * ER, ND], [1, ER]]), s_ld)
    v_preload = s_ld.v
    v_op("memset", SCA[:, 16704:17408], 0.0)
    p.drain("vector")
    v_orf0 = p.dma("gpsimd", ORFT[:, :], APx(orft_d, 0, [[MF, 128], [1, MF]]), s_gld)
    v_op("tensor_scalar_mul", ORFT[:, :], ORFT[:, :], S1A, wait=[(s_gld, v_orf0)])
    p.drain("vector")
    for c in range(ND):
        v_op("tensor_scalar_mul", STAT[:, ER * c:ER * c + ER],
             STAT[:, ER * c:ER * c + ER], WLN2[:, c:c + 1],
             wait=([(s_ld, v_preload), (s_ld, v_wrt)] if c == 0 else ()))
    p.drain("vector")
    v_op("tensor_copy", WROUT[:, :], STAT[:, 0:ND * ER])
    p.drain("vector")
    v_op("tensor_sub", WRL[:, :], STAT[:, 0:ND * ER], WROUT[:, :])
    p.drain("vector")
    v_orf = s_dv.v

    def cast_load(dst_ap, src_t, rows, cols, sem, wait=(), row_off=0):
        nch = rows // 128
        if nch <= 4:
            src = APx(src_t, row_off * cols, [[cols, 128], [128 * cols, nch], [1, cols]])
            return p.dma("gpsimd", dst_ap, src, sem, wait=wait)
        half = nch // 2
        t0 = dst_ap.tensor if hasattr(dst_ap, "tensor") else None
        src1 = APx(src_t, row_off * cols, [[cols, 128], [128 * cols, half], [1, cols]])
        src2 = APx(src_t, (row_off + half * 128) * cols,
                   [[cols, 128], [128 * cols, nch - half], [1, cols]])
        d1 = bass.AP(dst_ap.tensor, dst_ap.offset,
                     [list(dst_ap.ap[0]), [1, half * cols]])
        d2 = bass.AP(dst_ap.tensor, dst_ap.offset + half * cols,
                     [list(dst_ap.ap[0]), [1, (nch - half) * cols]])
        p.dma("gpsimd", d1, src1, sem, wait=wait)
        return p.dma("gpsimd", d2, src2, sem)

    def norm_stats(kind, wait_first=(), use_rt=False, fuse_rms=False):
        if not use_rt:
            wait_first = list(wait_first) + [(s_dv, s_dv.v), (s_ac, s_ac.v),
                                            (s_mm, s_mm.v), (s_gld, s_gld.v)]
        for i in range(NT):
            xc = R[:, i * D:(i + 1) * D]
            if use_rt:
                w = p.use(f"rt{i}") + (list(wait_first) if i == 0 else [])
            else:
                w = list(wait_first) if i == 0 else []
            wslot = p.use(f"xds{i % 2}")
            a_op("activation", out=SCF[:, (i % 2) * D:(i % 2) * D + D], in_=xc,
                 func=AF.Square, accum_out=STAT[:, i:i + 1], wait=w + wslot)
            p.drain("scalar")
            if kind == "ln":
                v_op("reduce_sum", STAT[:, 16 + i:17 + i], xc, axis=AX, wait=w)
        vq = s_ac.v
        if kind == "ln":
            p.drain("vector")
            v_op("tensor_scalar_mul", STAT[:, 32:48], STAT[:, 16:32], 1.0 / D)
            p.drain("vector")
            v_op("tensor_mul", STAT[:, 64:80], STAT[:, 32:48], STAT[:, 32:48])
            p.drain("vector")
            v_op("tensor_scalar", STAT[:, 48:64], STAT[:, 0:16], 1.0 / D, None,
                 ALU.mult, wait=[(s_ac, vq)])
            p.drain("vector")
            v_op("tensor_sub", STAT[:, 48:64], STAT[:, 48:64], STAT[:, 64:80])
            vv = s_dv.v
            a_op("activation", out=STAT[:, 64:80], in_=STAT[:, 48:64], func=AF.Sqrt,
                 bias=CONST[:, 4:5], wait=[(s_dv, vv)])
            va = s_ac.v
            v_op("reciprocal", STAT[:, 80:96], STAT[:, 64:80], wait=[(s_ac, va)])
            if fuse_rms:
                a_op("activation", out=STAT[:, 0:16], in_=STAT[:, 48:64],
                     func=AF.Sqrt, wait=[(s_dv, s_dv.v)])
                vaf = s_ac.v
                v_op("scalar_tensor_tensor", STAT[:, 16:32], STAT[:, 64:80],
                     RMS_EPS, STAT[:, 0:16], ALU.mult, ALU.add,
                     wait=[(s_ac, vaf)])
                p.drain("vector")
                v_op("reciprocal", STAT[:, 0:16], STAT[:, 16:32])
                p.drain("vector")
                v_op("tensor_mul", STAT[:, 16:32], STAT[:, 64:80], STAT[:, 0:16])
                p.drain("vector")
                v_op("tensor_mul", STAT[:, 96:112], STAT[:, 80:96], STAT[:, 16:32])
        else:
            a_op("activation", out=STAT[:, 64:80], in_=STAT[:, 0:16], func=AF.Sqrt,
                 scale=1.0 / D, wait=[(s_ac, vq)])
            va = s_ac.v
            v_op("tensor_scalar_add", STAT[:, 48:64], STAT[:, 64:80], RMS_EPS,
                 wait=[(s_ac, va)])
            p.drain("vector")
            v_op("reciprocal", STAT[:, 80:96], STAT[:, 48:64])
        p.drain("vector")
        return s_dv.v

    def norm_apply_fused(v_stats, spill):
        v_sp = big_wdma("sync", sp_x, R, s_st, wait=[(s_dv, v_stats)]) if spill else 0
        vsc = s_sc.v
        for i in range(NT):
            xc = R[:, i * D:(i + 1) * D]
            slot = SCF[:, (i % 2) * D:(i % 2) * D + D]
            w = [(s_dv, v_stats), (s_ac, s_ac.v)] + p.use(f"xds{i % 2}")
            if spill:
                w.append((s_st, v_sp))
            v_op("tensor_scalar", slot, xc, STAT[:, 32 + i:33 + i],
                 STAT[:, 96 + i:97 + i], ALU.subtract, ALU.mult, wait=w)
            v_slot = s_dv.v
            p.drain("vector")
            v_op("tensor_scalar", xc, xc, STAT[:, 32 + i:33 + i],
                 STAT[:, 80 + i:81 + i], ALU.subtract, ALU.mult)
            xnt_tile_transpose(i, v_slot,
                               w_extra=([(s_sc, vsc)] if i == 0 else ()))
            p.release(f"xds{i % 2}", s_mm)
        p.drain("vector")
        return s_dv.v

    def norm_apply_dump(kind, v_stats, spill=False, dump=False, lo_dump=False):
        vw_pre = [(s_ac, s_ac.v)]
        v_sp = 0
        vsc = s_sc.v
        if spill:
            v_sp = big_wdma("sync", sp_x, R, s_st, wait=[(s_dv, v_stats)])
        for i in range(NT):
            xc = R[:, i * D:(i + 1) * D]
            slot = SCF[:, (i % 2) * D:(i % 2) * D + D]
            w = [(s_dv, v_stats)] + p.use(f"xds{i % 2}") + vw_pre
            if spill:
                w.append((s_st, v_sp))
            if kind == "ln":
                v_op("tensor_scalar", slot, xc, STAT[:, 32 + i:33 + i],
                     STAT[:, 80 + i:81 + i], ALU.subtract, ALU.mult, wait=w)
            else:
                v_op("tensor_scalar_mul", slot, xc, STAT[:, 80 + i:81 + i], wait=w)
            v_slot = s_dv.v
            if lo_dump:
                loslot = SCB[:, (i % 2) * D:(i % 2) * D + D]
                wlo = p.use(f"lo{i % 2}")
                if i < 2:
                    wlo = wlo + p.use("ytk0") + p.use("ytk1")
                p.drain("vector")
                v_op("scalar_tensor_tensor", loslot, xc, STAT[:, 80 + i:81 + i],
                     slot, ALU.mult, ALU.subtract, wait=wlo)
            vm_tp = xnt_tile_transpose(i, v_slot,
                                       w_extra=([(s_sc, vsc)] if i == 0 else ()))
            if dump:
                vd = s_dv.v
                p.dma("sync", APx(d_xn, i * 128 * D, [[D, 128], [1, D]]), slot,
                      s_pp[i % 2], wait=[(s_dv, vd), (s_mm, vm_tp)])
                p.release(f"xds{i % 2}", s_pp[i % 2])
                if lo_dump:
                    p.dma("sync", APx(d_lo, i * 128 * D, [[D, 128], [1, D]]), loslot,
                          s_pp[2 + i % 2], wait=[(s_dv, vd)])
                    p.release(f"lo{i % 2}", s_pp[2 + i % 2])
            else:
                p.release(f"xds{i % 2}", s_mm)
        p.drain("vector")
        return (s_pp[0].v, s_pp[1].v, s_pp[2].v, s_pp[3].v, s_dv.v)

    def xnt_read(wait_pp, gain=None):
        vals = []
        for c in range(ND):
            src = APx(d_xn, c * 128, [[D, S], [1, 128]])
            w = [(s_pp[0], wait_pp[0]), (s_pp[1], wait_pp[1]),
                 (s_mm, s_mm.v)] if c == 0 else ()
            vals.append(p.dma("sync", XNT[:, c * S:(c + 1) * S], src, s_tp,
                              wait=w, transpose=True))
        v_all = vals[-1]
        if gain is not None:
            for c in range(ND):
                v_op("tensor_scalar_mul", XNT[:, c * S:(c + 1) * S],
                     XNT[:, c * S:(c + 1) * S], gain[:, c:c + 1],
                     wait=([(s_tp, v_all)] if c == 0 else ()))
            p.drain("vector")
        return v_all, s_dv.v

    def xnt_tile_transpose(i, v_slot, w_extra=()):
        slot = SCF[:, (i % 2) * D:(i % 2) * D + D]
        bks = (0, 1) if i % 2 == 0 else (2, 3)
        vals = []
        for half in range(2):
            bk = bks[half]
            w0 = p.use(f"ps{bk}") + [(s_dv, v_slot), (s_ld, v_preload)]
            for j in range(4):
                c = 4 * half + j
                mm(PS[bk][:, 128 * j:128 * j + 128],
                   slot[:, 128 * c:128 * c + 128], IDENTB[:, :],
                   start=True, stop=True, wait=(w0 if j == 0 else ()))
            vals.append(s_mm.v)
        for half in range(2):
            bk = bks[half]
            src = APx(PS[bk], 0, [[512, 128], [128, 4], [1, 128]])
            dst = APx(XNT, S * 4 * half + 128 * i, [[NT * D, 128], [S, 4], [1, 128]])
            v_op("tensor_copy", dst, src,
                 wait=[(s_mm, vals[half])] + list(w_extra if half == 0 else ()))
            p.release(f"ps{bk}", s_dv)
        return s_mm.v

    def xnt_gain(gain):
        for c in range(ND):
            v_op("tensor_scalar_mul", XNT[:, c * S:(c + 1) * S],
                 XNT[:, c * S:(c + 1) * S], gain[:, c:c + 1])
        p.drain("vector")
        return s_dv.v

    def dump_dbg():
        wdbg = [(s_dv, s_dv.v), (s_ac, s_ac.v), (s_mm, s_mm.v), (s_gld, s_gld.v)]
        for i_ in range(NT):
            wdbg += p.use(f"rt{i_}")
        big_wdma("sync", dbg_t, R, s_dbg, wait=wdbg)
        p.wait("gpsimd", (s_dbg, s_dbg.v))

    def phase_attn_a(v_wq, v_wk, v_wv):
        vst = norm_stats("ln", wait_first=[(s_ld, v_preload)])
        v_xnt = norm_apply_dump("ln", vst)[4]

        eqo, eko, kvo = 8320, 12416, 16512

        def build_v_half_a(half, wait_extra):
            v_op("memset", APx(SCA, 64, [[17408, 128], [520, NT], [65, 8], [1, 1]]),
                 1.0, wait=wait_extra)
            v_ones = s_dv.v
            for l in range(NT):
                bk = l % 2
                bank = PS[bk]
                w0 = p.use(f"ps{bk}") + [(s_dv, v_ones)]
                if half == 0 and l == 0:
                    w0 += [(s_dv, v_xnt), (s_wl[2], v_wv)]
                for c in range(ND):
                    mm(bank[:, :], XNT[:, c * S + 128 * l: c * S + 128 * l + 128],
                       WB2[:, c * D + half * 512: c * D + half * 512 + 512],
                       start=(c == 0), stop=False, wait=(w0 if c == 0 else ()))
                mm(bank[:, :], CONSTB[0:1, 0:128],
                   CROW[0:1, half * 512: half * 512 + 512], start=False, stop=True)
                vm = s_mm.v
                dst = APx(SCA, 520 * l, [[17408, 128], [65, 8], [1, 64]])
                src = APx(PS[bk], 0, [[512, 128], [64, 8], [1, 64]])
                v_op("tensor_copy", dst, src, wait=[(s_mm, vm)])
                p.release(f"ps{bk}", s_dv)
            p.drain("vector")
            return s_dv.v

        v_vdone = 0
        v_wo = None
        for pp_i in range(ND):
            if pp_i == 0:
                v_vdone = build_v_half_a(0, [])
            elif pp_i == 4:
                v_vdone = build_v_half_a(1, [(s_mm, s_mm.v)])
                v_wo = cast_load(WB2[:, :], woa, D, D, s_wl[2], wait=[(s_mm, s_mm.v)])

            for (wslot, swl, vw, bias, dsto) in ((WB0, s_wl[0], v_wq, BQA, 0),
                                                 (WB1, s_wl[1], v_wk, BKA, 2048)):
                for lb in range(4):
                    bk = 2 + (lb % 2)
                    bank = PS[bk]
                    wv = p.use(f"ps{bk}") + [(s_dv, v_xnt), (swl, vw)]
                    for c in range(ND):
                        mm(bank[:, :],
                           wslot[:, c * D + 128 * pp_i: c * D + 128 * pp_i + 128],
                           XNT[:, c * S + 512 * lb: c * S + 512 * lb + 512],
                           start=(c == 0), stop=(c == ND - 1),
                           wait=(wv if c == 0 else ()))
                    vm = s_mm.v
                    v_op("tensor_scalar_add",
                         SCB[:, dsto + 512 * lb: dsto + 512 * lb + 512],
                         bank[:, :], bias[:, pp_i:pp_i + 1], wait=[(s_mm, vm)])
                    p.release(f"ps{bk}", s_dv)
            if pp_i == ND - 1:
                p.release("wb0", s_mm)
                p.release("wb1", s_mm)
            p.drain("vector")
            v_qk = s_dv.v

            for hh in range(2):
                gh = 2 * pp_i + hh
                hloc = gh % 8
                hsl = slice(64 * hh, 64 * hh + 64)
                onescol = CONSTB[hsl, 128:129]
                v_op("tensor_mul", SCB[:, 4096:6144], SCB[:, 2048:4096],
                     SCB[:, 2048:4096], wait=[(s_dv, v_qk), (s_mm, s_mm.v)])
                p.drain("vector")
                v_sk = s_dv.v
                w4 = p.use("ps4")
                for l in range(NT):
                    mm(PS[4][:, l:l + 1], SCB[hsl, 4096 + 128 * l:4224 + 128 * l],
                       onescol, start=True, stop=True,
                       wait=(w4 + [(s_dv, v_sk)] if l == 0 else ()))
                vm_nk = s_mm.v
                v_op("tensor_scalar", STAT[:, 0:16], PS[4][:, 0:16], -S2A,
                     -math.log(MF) / 2.0, ALU.mult, ALU.add,
                     wait=[(s_mm, vm_nk), (s_ac, s_ac.v)])
                p.drain("vector")
                p.release("ps4", s_dv)
                v_nk = s_dv.v

                a_op("activation", out=STAT[:, 16:32], in_=STAT[:, 0:16],
                     func=AF.Exp, wait=[(s_dv, v_nk)])
                v_eb = s_ac.v
                for lp in range(8):
                    bk = 6 + (lp % 2)
                    bank = PS[bk]
                    w0 = p.use(f"ps{bk}")
                    if lp == 0:
                        w0 = w0 + [(s_dv, v_nk), (s_dv, v_orf)]
                    for j in range(2):
                        l = 2 * lp + j
                        mm(bank[:, 256 * j:256 * j + MF],
                           SCB[hsl, 2048 + 128 * l:2176 + 128 * l],
                           ORFT[hsl, :], start=True, stop=True,
                           wait=(w0 if j == 0 else ()))
                    vm = s_mm.v
                    a_op("activation",
                         out=SCA[:, eko + 512 * lp:eko + 512 * lp + 512],
                         in_=bank[:, 0:512], func=AF.Exp,
                         wait=[(s_mm, vm), (s_ac, v_eb)])
                    va = s_ac.v
                    for j in range(2):
                        l = 2 * lp + j
                        v_op("tensor_scalar",
                             SCA[:, eko + MF * l:eko + MF * l + MF],
                             SCA[:, eko + MF * l:eko + MF * l + MF],
                             STAT[:, 16 + l:17 + l], EPSP, ALU.mult, ALU.add,
                             wait=[(s_ac, va)])
                    p.release(f"ps{bk}", s_dv)
                p.drain("vector")
                v_ek = s_dv.v

                for mc2 in range(2):
                    bk = 4 + mc2
                    bank = PS[bk]
                    w0 = p.use(f"ps{bk}") + [(s_dv, v_ek), (s_dv, v_vdone)]
                    for l in range(NT):
                        mm(bank[:, 0:65],
                           SCA[:, eko + MF * l + 128 * mc2:
                               eko + MF * l + 128 * mc2 + 128],
                           SCA[:, 520 * l + 65 * hloc: 520 * l + 65 * hloc + 65],
                           start=(l == 0), stop=(l == NT - 1),
                           wait=(w0 if l == 0 else ()))
                    vm = s_mm.v
                    v_op("tensor_copy", SCA[:, kvo + 65 * mc2:kvo + 65 * mc2 + 65],
                         bank[:, 0:65], wait=[(s_mm, vm)])
                    p.release(f"ps{bk}", s_dv)
                p.drain("vector")
                v_kv = s_dv.v

                w2 = p.use("ps2")
                for mc2 in range(2):
                    mm(PS[2][0:1, 0:65], CONSTB[:, 128:129],
                       SCA[:, kvo + 65 * mc2:kvo + 65 * mc2 + 65],
                       start=(mc2 == 0), stop=(mc2 == 1),
                       wait=(w2 + [(s_dv, v_kv)] if mc2 == 0 else ()))
                vm_c = s_mm.v
                v_op("tensor_copy", SCA[0:1, 16642:16707], PS[2][0:1, 0:65],
                     wait=[(s_mm, vm_c)])
                p.drain("vector")
                p.release("ps2", s_dv)
                v_c = s_dv.v

                v_op("tensor_mul", SCB[:, 4096:6144], SCB[:, 0:2048], SCB[:, 0:2048],
                     wait=[(s_mm, s_mm.v)])
                p.drain("vector")
                v_sq = s_dv.v
                for lb in range(4):
                    bk = 2 + (lb % 2)
                    w0 = p.use(f"ps{bk}") + [(s_dv, v_sq), (s_ac, s_ac.v)]
                    mm(PS[bk][0:1, :], onescol,
                       SCB[hsl, 4096 + 512 * lb:4608 + 512 * lb],
                       start=True, stop=True, wait=w0)
                    vm = s_mm.v
                    a_op("activation", out=SCF[0:1, 512 * lb:512 * lb + 512],
                         in_=PS[bk][0:1, :], func=AF.Exp, scale=S2A,
                         bias=CONST[0:1, 5:6], wait=[(s_mm, vm)])
                    p.release(f"ps{bk}", s_ac)
                v_g = s_ac.v

                v_eq_lb = {}
                for lb in range(4):
                    for mc2 in range(2):
                        bk = 6 + ((2 * lb + mc2) % 2)
                        w0 = p.use(f"ps{bk}") + [(s_dv, v_sq)]
                        mm(PS[bk][:, :], ORFT[hsl, 128 * mc2:128 * mc2 + 128],
                           SCB[hsl, 512 * lb:512 * lb + 512], start=True, stop=True,
                           wait=w0)
                        vm = s_mm.v
                        a_op("activation",
                             out=SCA[:, eqo + 2048 * mc2 + 512 * lb:
                                     eqo + 2048 * mc2 + 512 * lb + 512],
                             in_=PS[bk][:, :], func=AF.Exp, wait=[(s_mm, vm)])
                        p.release(f"ps{bk}", s_ac)
                    v_eq_lb[lb] = s_ac.v
                v_eq = s_ac.v

                for lp in range(8):
                    vms = []
                    for j in range(2):
                        l = 2 * lp + j
                        bank = PS[j]
                        w0 = (p.use(f"ps{j}") +
                              [(s_ac, v_eq_lb[l // 4]), (s_dv, v_c), (s_ac, v_g)])
                        for mc2 in range(2):
                            mm(bank[:, 0:65],
                               SCA[:, eqo + 2048 * mc2 + 128 * l:
                                   eqo + 2048 * mc2 + 128 * l + 128],
                               SCA[:, kvo + 65 * mc2:kvo + 65 * mc2 + 65],
                               start=(mc2 == 0), stop=False,
                               wait=(w0 if mc2 == 0 else ()))
                        mm(bank[:, 0:65], SCF[0:1, 128 * l:128 * l + 128],
                           SCA[0:1, 16642:16707], start=False, stop=True)
                        vms.append(s_mm.v)
                    v_op("reciprocal", SM0[:, 0:1], PS[0][:, 64:65],
                         wait=[(s_mm, vms[0])])
                    v_op("reciprocal", SM0[:, 1:2], PS[1][:, 64:65],
                         wait=[(s_mm, vms[1])])
                    p.drain("vector")
                    for j in range(2):
                        l = 2 * lp + j
                        v_op("tensor_scalar_mul",
                             SCB[:, 6144 + 128 * l + 64 * hh:
                                 6144 + 128 * l + 64 * hh + 64],
                             PS[j][:, 0:64], SM0[:, j:j + 1],
                             wait=(p.use("pairout") if j == 0 else ()))
                    p.drain("vector")
                    p.release("ps0", s_dv)
                    p.release("ps1", s_dv)
            vdp = s_dv.v
            for g_ in range(4):
                p.dma("sync",
                      APx(d_ya, 128 * pp_i + g_ * 4 * 128 * D,
                          [[D, 128], [128 * D, 4], [1, 128]]),
                      APx(SCB, 6144 + g_ * 4 * 128, [[8192, 128], [128, 4], [1, 128]]),
                      s_pp[2], wait=([(s_dv, vdp)] if g_ == 0 else ()))
            p.release("pairout", s_pp[2])
        v_pairs = s_pp[2].v

        tv = []
        for c in range(ND):
            src = APx(d_ya, c * 128, [[D, S], [1, 128]])
            w = [(s_pp[2], v_pairs), (s_mm, s_mm.v), (s_dv, s_dv.v)] if c == 0 else ()
            tv.append(p.dma("sync", SCA[:, c * S:(c + 1) * S], src, s_tp, wait=w,
                            transpose=True))
        v_ya = tv[-1]
        for l in range(NT):
            for hf in range(2):
                bk = 2 * (l % 2) + hf
                bank = PS[bk]
                w0 = p.use(f"ps{bk}") + [(s_tp, v_ya), (s_wl[2], v_wo)]
                for c in range(ND):
                    mm(bank[:, :], SCA[:, c * S + 128 * l: c * S + 128 * l + 128],
                       WB2[:, c * D + 512 * hf: c * D + 512 * hf + 512],
                       start=(c == 0), stop=False, wait=(w0 if c == 0 else ()))
                mm(bank[:, :], CONSTB[0:1, 0:128],
                   CROW[0:1, 1024 + 512 * hf:1536 + 512 * hf], start=False, stop=True)
                vm = s_mm.v
                dst = R[:, l * D + 512 * hf: l * D + 512 * hf + 512]
                v_op("tensor_tensor", dst, bank[:, :], dst, ALU.add,
                     wait=[(s_mm, vm)])
                if hf == 1:
                    p.release(f"rt{l}", s_dv)
                p.release(f"ps{bk}", s_dv)
        p.release("wb2", s_mm)
        p.drain("vector")

    def phase_mla(v_xnt, v_gain):
        vmm0 = s_mm.v

        def vcol(l):
            return 520 * l if l < 15 else 16512

        vk0 = cast_load(WB1[:, 4096:6144], wkvd, D, LAT, s_wl[1],
                        wait=p.use("wb1"))
        vl_lat = cast_load(WB1[:, 6144:8192], wqd, D, LAT, s_wl[1],
                           wait=[(s_wl[1], vk0)])
        vl_wb1 = cast_load(WB1[:, 0:4096], wropek, D, 512, s_wl[1],
                           wait=[(s_wl[1], vl_lat)])
        vl_wku = cast_load(WB2[:, 0:2048], wku_p, LAT, 1024, s_wl[2],
                           wait=p.use("wb2"))
        vl_wqc = cast_load(WB2[:, 2048:4096], wq_c, LAT, 1024, s_wl[2],
                           wait=[(s_wl[2], vl_wku)])
        vl_wb2 = cast_load(WB2[:, 4096:6144], wvu, LAT, D, s_wl[2],
                           wait=[(s_wl[2], vl_wqc)])
        p.dma("sync", WB0[:, 0:2048], APx(ropec, 0, [[S, 128], [1, S]]), s_tb,
              wait=p.use("wb0"))
        p.dma("sync", WB0[:, 2048:4096], APx(ropes, 0, [[S, 128], [1, S]]), s_tb)
        p.dma("sync", WB0[:, 4096:6144], APx(ropec2, 0, [[S, 128], [1, S]]), s_tb)
        vl_tbl = p.dma("sync", WB0[:, 6144:8192], APx(ropes2, 0, [[S, 128], [1, S]]),
                       s_tb)

        for (wb, woff, swl, wv_, dsto) in ((WB1, 4096, s_wl[1], vl_lat, 0),
                                           (WB1, 6144, s_wl[1], vl_lat, 4096)):
            for ic in range(2):
                for lb in range(4):
                    bk = lb % 2
                    bank = PS[bk]
                    w0 = p.use(f"ps{bk}") + [(s_dv, v_xnt), (s_dv, v_gain), (swl, wv_)]
                    for c in range(ND):
                        mm(bank[:, :],
                           wb[:, woff + c * LAT + 128 * ic:
                              woff + c * LAT + 128 * ic + 128],
                           XNT[:, c * S + 512 * lb: c * S + 512 * lb + 512],
                           start=(c == 0), stop=(c == ND - 1),
                           wait=(w0 if c == 0 else ()))
                    vm = s_mm.v
                    v_op("tensor_copy",
                         SCB[:, dsto + 2048 * ic + 512 * lb:
                             dsto + 2048 * ic + 512 * lb + 512],
                         bank[:, :], wait=[(s_mm, vm)])
                    p.release(f"ps{bk}", s_dv)
        v_lat = s_dv.v
        if stop_after == "mla_lat":
            dump_dbg()
            return True

        for h2 in range(2):
            for c2 in range(2):
                hc = 2 * h2 + c2
                kro = 8320 + 4096 * h2 + 2048 * c2
                for lb in range(4):
                    bk = 2 + (lb % 2)
                    bank = PS[bk]
                    w0 = p.use(f"ps{bk}") + [(s_dv, v_xnt), (s_dv, v_gain),
                                             (s_wl[1], vl_wb1)]
                    for c in range(ND):
                        mm(bank[:, :],
                           WB1[:, c * 512 + 128 * hc: c * 512 + 128 * hc + 128],
                           XNT[:, c * S + 512 * lb: c * S + 512 * lb + 512],
                           start=(c == 0), stop=(c == ND - 1),
                           wait=(w0 if c == 0 else ()))
                    vm = s_mm.v
                    v_op("tensor_scalar_mul",
                         SCA[:, kro + 512 * lb: kro + 512 * lb + 512],
                         bank[:, :], SQ8, wait=[(s_mm, vm)])
                    p.release(f"ps{bk}", s_dv)
        p.drain("vector")
        v_raws = s_dv.v
        if stop_after == "mla_raws":
            dump_dbg()
            return True

        p.release("wb1", s_mm)
        mask32 = [(i + 16) % 32 for i in range(32)]
        vl_wo = None
        v_yq = {}
        vmm_latdone = 0
        for h2 in range(2):
            v_scpre = s_sc.v
            for pr in range(4):
                gp = 4 * h2 + pr
                for lb in range(4):
                    bk = lb % 2
                    bank = PS[bk]
                    w0 = p.use(f"ps{bk}") + [(s_dv, v_lat), (s_wl[2], vl_wku)]
                    for ic in range(2):
                        mm(bank[:, :],
                           WB2[:, ic * 1024 + 128 * gp: ic * 1024 + 128 * gp + 128],
                           SCB[:, 2048 * ic + 512 * lb: 2048 * ic + 512 * lb + 512],
                           start=(ic == 0), stop=(ic == 1),
                           wait=(w0 if ic == 0 else ()))
                    vm = s_mm.v
                    v_op("tensor_scalar_mul",
                         XNT[:, 2048 * pr + 512 * lb: 2048 * pr + 512 * lb + 512],
                         bank[:, :], SQ8,
                         wait=[(s_mm, vm)] + ([(s_sc, v_scpre)]
                                              if pr == 0 and lb == 0 else []))
                    p.release(f"ps{bk}", s_dv)
            p.drain("vector")
            v_knope = s_dv.v

            v_scpre2 = s_sc.v
            for c2 in range(2):
                kro = 8320 + 4096 * h2 + 2048 * c2
                raw = SCA[:, kro: kro + 2048]
                shuf = SCF[:, 0:2048]
                w0 = [(s_dv, v_raws), (s_tb, vl_tbl), (s_mm, s_mm.v),
                      (s_ac, s_ac.v)]
                w0 += (p.use("slab") + p.use("pt0") + p.use("pt1")
                       + p.use("pt2") + p.use("stg") + [(s_pp[3], s_pp[3].v)])
                v_op("stream_shuffle", shuf, raw, mask=mask32, wait=w0)
                p.drain("vector")
                v_op("tensor_mul", shuf, shuf, WB0[:, 2048:4096])
                p.drain("vector")
                v_op("tensor_mul", raw, raw, WB0[:, 0:2048])
                p.drain("vector")
                v_op("tensor_add", raw, raw, shuf)
                p.drain("vector")
                p.release("slab", s_dv)
                v_kap = s_dv.v
                for j in range(4):
                    hlk = 4 * c2 + j
                    prk = hlk // 2
                    ro = 32 + 64 * (hlk % 2)
                    wsc = ([(s_dv, v_kap), (s_dv, v_knope),
                            (s_sc, v_scpre2)] if j == 0 else ())
                    p.dma("sync", XNT[ro:ro + 32, 2048 * prk: 2048 * prk + 2048],
                          SCA[32 * j:32 * j + 32, kro: kro + 2048], s_sc, wait=wsc)
            v_scat_all = s_sc.v

            if h2 == 0:
                wkz = ([(s_sc, v_scat_all), (s_mm, s_mm.v)]
                       + p.use("kst0") + p.use("kst1"))
                v_op("memset", SCA[64:128, 8320:10368], 0.0, wait=wkz)
                v_op("memset", SCA[0:64, 10368:12416], 0.0)
                p.drain("vector")
                v_kz = s_dv.v

            for pr in range(4):
                gp = 4 * h2 + pr
                for lb in range(4):
                    bk = 2 + (lb % 2)
                    bank = PS[bk]
                    w0 = p.use(f"ps{bk}") + [(s_dv, v_lat), (s_wl[2], vl_wqc)]
                    for ic in range(2):
                        mm(bank[:, :],
                           WB2[:, 2048 + ic * 1024 + 128 * gp:
                               2048 + ic * 1024 + 128 * gp + 128],
                           SCB[:, 4096 + 2048 * ic + 512 * lb:
                               4096 + 2048 * ic + 512 * lb + 512],
                           start=(ic == 0), stop=(ic == 1),
                           wait=(w0 if ic == 0 else ()))
                    vm = s_mm.v
                    v_op("tensor_scalar_mul",
                         XNT[:, 8192 + 2048 * pr + 512 * lb:
                             8192 + 2048 * pr + 512 * lb + 512],
                         bank[:, :], SQ8, wait=[(s_mm, vm)])
                    p.release(f"ps{bk}", s_dv)
                p.drain("vector")
                qsl = XNT[:, 8192 + 2048 * pr: 8192 + 2048 * pr + 2048]
                shuf = SCF[:, 0:2048]
                w0 = ([(s_tb, vl_tbl)] + p.use("slab") + p.use("pt0")
                      + p.use("pt1") + p.use("pt2") + p.use("stg")
                      + [(s_pp[3], s_pp[3].v)])
                v_op("stream_shuffle", shuf, qsl, mask=mask32, wait=w0)
                p.drain("vector")
                v_op("tensor_mul", shuf, shuf, WB0[:, 6144:8192])
                p.drain("vector")
                v_op("tensor_mul", qsl, qsl, WB0[:, 4096:6144])
                p.drain("vector")
                v_op("tensor_add", qsl, qsl, shuf)
                p.drain("vector")
                p.release("slab", s_dv)
            v_kkqq = s_dv.v

            if h2 == 1 and vl_wo is None:
                vl_wo = cast_load(WB0[:, :], wo_m, D, D, s_wl[0], wait=[(s_dv, v_kkqq)])

            v_op("memset", APx(SCA, 64, [[17408, 128], [520, 15], [65, 8], [1, 1]]),
                 1.0, wait=[(s_dv, v_raws), (s_mm, s_mm.v)])
            v_op("memset", APx(SCA, 16512 + 64, [[17408, 128], [65, 8], [1, 1]]), 1.0)
            v_ones2 = s_dv.v
            for l in range(NT):
                bk = 4 + (l % 2)
                bank = PS[bk]
                w0 = p.use(f"ps{bk}") + [(s_dv, v_lat), (s_wl[2], vl_wb2),
                                         (s_dv, v_ones2)]
                for ic in range(2):
                    mm(bank[:, :],
                       SCB[:, 2048 * ic + 128 * l: 2048 * ic + 128 * l + 128],
                       WB2[:, 4096 + 1024 * ic + 512 * h2:
                           4096 + 1024 * ic + 512 * h2 + 512],
                       start=(ic == 0), stop=(ic == 1), wait=(w0 if ic == 0 else ()))
                vm = s_mm.v
                dst = APx(SCA, vcol(l), [[17408, 128], [65, 8], [1, 64]])
                src = APx(PS[bk], 0, [[512, 128], [64, 8], [1, 64]])
                v_op("tensor_copy", dst, src, wait=[(s_mm, vm)])
                p.release(f"ps{bk}", s_dv)
            if h2 == 1:
                p.release("wb2", s_mm)
                vmm_latdone = s_mm.v
            p.drain("vector")
            v_vh = s_dv.v
            if stop_after == f"mla_v{h2}":
                dump_dbg()
                return True

            v_stg_pair = 0
            pair_vmh = 0
            for hl in range(8):
                gh = 8 * h2 + hl
                pr = hl // 2
                par = hl % 2
                if par == 0:
                    for pj in range(2):
                        w0 = (p.use(f"kst{pj}")
                              + [(s_dv, v_kz), (s_dv, v_knope), (s_dv, v_kkqq),
                                 (s_sc, v_scat_all)])
                        p.dma("sync",
                              SCA[64 * pj:64 * pj + 64,
                                  8320 + 2048 * pj: 8320 + 2048 * pj + 2048],
                              XNT[64 * pj:64 * pj + 64, 2048 * pr: 2048 * pr + 2048],
                              s_sc, wait=w0)
                    v_stg_pair = s_sc.v
                    kslab = XNT[:, 2048 * pr: 2048 * pr + 2048]
                    qslab = XNT[:, 8192 + 2048 * pr: 8192 + 2048 * pr + 2048]
                    kbank = (6, 7, 0, 1)
                    wsl = (p.use("slab") + p.use("pt0") + p.use("pt1")
                           + p.use("pt2") + p.use("stg")
                           + [(s_dv, v_vh), (s_sc, v_scat_all)])
                    v_op("tensor_mul", SCF[:, 0:2048], kslab, kslab, wait=wsl)
                    p.drain("vector")
                    v_sqk = s_dv.v
                    for kb in range(4):
                        bk = kbank[kb]
                        mm(PS[bk][:, :], CONSTB[:, 130:258],
                           SCF[:, 512 * kb: 512 * kb + 512], start=True, stop=True,
                           wait=p.use(f"ps{bk}") + [(s_dv, v_sqk)])
                    vmm_k = s_mm.v
                    for kb in range(4):
                        v_op("reduce_max", SM0[:, kb:kb + 1], PS[kbank[kb]][:, :],
                             axis=AX, wait=([(s_mm, vmm_k)] if kb == 0 else ()))
                    p.drain("vector")
                    v_op("reduce_max", SM0[:, 4:5], SM0[:, 0:4], axis=AX)
                    p.drain("vector")
                    v_mk = s_dv.v
                    for kb in range(4):
                        p.release(f"ps{kbank[kb]}", s_dv)
                    v_op("tensor_mul", SCF[:, 0:2048], qslab, qslab,
                         wait=[(s_mm, vmm_k)])
                    p.drain("vector")
                    v_sqq = s_dv.v
                    for qb in range(4):
                        bk = kbank[qb]
                        mm(PS[bk][:, :], CONSTB[:, 130:258],
                           SCF[:, 512 * qb: 512 * qb + 512], start=True, stop=True,
                           wait=p.use(f"ps{bk}") + [(s_dv, v_sqq)])
                    p.release("slab", s_mm)
                    vmm_q = s_mm.v
                    for qb in range(4):
                        v_op("reduce_max", SM0[:, qb:qb + 1], PS[kbank[qb]][:, :],
                             axis=AX, wait=([(s_mm, vmm_q), (s_dv, v_mk)]
                                            if qb == 0 else ()))
                    p.drain("vector")
                    v_op("tensor_scalar", SCF[:, 2048:2052], SM0[:, 0:4],
                         SM0[:, 4:5], None, ALU.add, wait=p.use("mhtot"))
                    p.drain("vector")
                    v_tot = s_dv.v
                    for qb in range(4):
                        p.release(f"ps{kbank[qb]}", s_dv)
                    w6 = p.use("ps6")
                    mm(PS[6][:, 0:4], CONSTB[:, 258:386], SCF[:, 2048:2052],
                       start=True, stop=True, wait=w6 + [(s_dv, v_tot)])
                    mm(PS[6][:, 4:8], CONSTB[:, 386:514], SCF[:, 2048:2052],
                       start=True, stop=True)
                    p.release("mhtot", s_mm)
                    vm_mh = s_mm.v
                    v_op("tensor_copy", SM0[:, 8:16], PS[6][:, 0:8],
                         wait=[(s_mm, vm_mh)] + p.use("mhcols"))
                    p.drain("vector")
                    p.release("ps6", s_dv)
                    pair_vmh = s_dv.v
                v_mh = pair_vmh
                v_stg = v_stg_pair
                ksl = SCA[:, 8320 + 2048 * par: 8320 + 2048 * par + 2048]

                for qb in range(4):
                    biascol = SM0[:, 8 + 4 * par + qb: 9 + 4 * par + qb]
                    w5 = p.use("ps5")
                    nkc = 4 * (qb + 1)
                    prev = None
                    for kc in range(nkc):
                        bk = (2, 3, 6)[kc % 3]
                        sbank = PS[bk]
                        w0 = p.use(f"ps{bk}") + [(s_dv, v_mh), (s_sc, v_stg)]
                        mm(sbank[:, :], ksl[:, 128 * kc: 128 * kc + 128],
                           XNT[:, 8192 + 2048 * pr + 512 * qb:
                               8192 + 2048 * pr + 512 * qb + 512],
                           start=True, stop=True, wait=w0)
                        vm = s_mm.v
                        if prev is not None:
                            pkc, pslot, v_pt = prev
                            mm(PS[5][:, :],
                               SCA[:, vcol(pkc) + 65 * hl:vcol(pkc) + 65 * hl + 128],
                               SCF[:, 512 * pslot:512 * pslot + 512],
                               start=(pkc == 0), stop=False,
                               wait=[v_pt, (s_dv, v_vh)] + (w5 if pkc == 0 else []))
                            p.release(f"pt{pslot}", s_mm)
                        pt = SCF[:, 512 * (kc % 3):512 * (kc % 3) + 512]
                        ptres = f"pt{kc % 3}"
                        wpt = p.use(ptres)
                        jst = 128 * kc - 512 * qb
                        if jst < 0:
                            a_op("activation", out=pt, in_=sbank[:, :], func=AF.Exp,
                                 bias=biascol, wait=[(s_mm, vm), (s_dv, v_mh)] + wpt)
                            v_pt = (s_ac, s_ac.v)
                        else:
                            if jst > 0:
                                v_op("memset", pt[:, 0:jst], 0.0, wait=wpt)
                            a_op("activation", out=pt[:, jst:512],
                                 in_=sbank[:, jst:512], func=AF.Exp, bias=biascol,
                                 wait=[(s_mm, vm), (s_dv, v_mh)] + (wpt if jst == 0 else []))
                            va = s_ac.v
                            v_op("tensor_mul", pt[:, jst:jst + 128],
                                 pt[:, jst:jst + 128], CONSTB[:, 0:128],
                                 wait=[(s_ac, va)])
                            p.drain("vector")
                            v_pt = (s_dv, s_dv.v)
                        p.release(f"ps{bk}", s_ac if jst < 0 else s_dv)
                        prev = (kc, kc % 3, v_pt)
                    pkc, pslot, v_pt = prev
                    mm(PS[5][:, :],
                       SCA[:, vcol(pkc) + 65 * hl:vcol(pkc) + 65 * hl + 128],
                       SCF[:, 512 * pslot:512 * pslot + 512],
                       start=(pkc == 0), stop=True,
                       wait=[v_pt, (s_dv, v_vh)] + (w5 if pkc == 0 else []))
                    p.release(f"pt{pslot}", s_mm)
                    if qb == 3:
                        p.release("mhcols", s_ac)
                        p.release(f"kst{par}", s_mm)
                    vm_pv = s_mm.v
                    wst = p.use("stg")
                    v_op("tensor_copy", SCF[0:65, 1536:2048], PS[5][0:65, :],
                         wait=[(s_mm, vm_pv)] + wst)
                    p.drain("vector")
                    p.release("ps5", s_dv)
                    vd = s_dv.v
                    p.dma("gpsimd", APx(d_yraw, 64 * gh * S + 512 * qb,
                                        [[S, 64], [1, 512]]),
                          SCF[0:64, 1536:2048], s_yd, wait=[(s_dv, vd)])
                    p.dma("gpsimd", APx(d_yraw, (1024 + gh) * S + 512 * qb,
                                        [[S, 1], [1, 512]]),
                          SCF[64:65, 1536:2048], s_yd)
                    p.release("stg", s_yd)
                    if h2 == 1 and hl == 7:
                        v_yq[qb] = s_yd.v
        v_yraw = s_yd.v
        if stop_after == "mla_flash":
            p.wait("sync", (s_yd, v_yraw))
            dump_dbg()
            return True

        v_y2h1 = None
        for l in range(NT):
            slot = SCB[:, (l % 4) * 1040:(l % 4) * 1040 + 1040]
            w = [(s_yd, v_yq[l // 4]), (s_mm, vmm_latdone)]
            w = list(w) + p.use(f"ytk{l % 4}")
            vt = p.dma("sync", slot, APx(d_yraw, 128 * l, [[S, 1040], [1, 128]]),
                       s_pp[l % 4], wait=w, transpose=True)
            v_op("reciprocal", STAT[:, 112:128], slot[:, 1024:1040],
                 wait=[(s_pp[l % 4], vt)])
            p.drain("vector")
            for h_ in range(H):
                v_op("tensor_scalar_mul", slot[:, 64 * h_:64 * h_ + 64],
                     slot[:, 64 * h_:64 * h_ + 64], STAT[:, 112 + h_:113 + h_])
            p.drain("vector")
            vd = s_dv.v
            p.dma("gpsimd", APx(d_y2, 128 * l * D, [[D, 128], [1, D]]),
                  slot[:, 0:1024], s_rl[l % 4], wait=[(s_dv, vd)])
            p.release(f"ytk{l % 4}", s_rl[l % 4])
            if l == 7:
                v_y2h1 = [(s_rl[i], s_rl[i].v) for i in range(4)]
        v_y2 = [(s_rl[i], s_rl[i].v) for i in range(4)]
        if stop_after == "mla_trip":
            dump_dbg()
            return True

        v_y2t_h1 = 0
        for half in range(2):
            for c in range(ND):
                src = APx(d_y2, half * 1024 * D + c * 128, [[D, 1024], [1, 128]])
                w = [(s_tp, s_tp.v)]
                if c == 0 and half == 0:
                    w += v_y2h1 + [(s_mm, vmm_latdone), (s_dv, s_dv.v)]
                elif c == 0:
                    w += v_y2
                p.dma("sync",
                      SCA[:, c * S + 1024 * half:c * S + 1024 * half + 1024],
                      src, s_tp, wait=w, transpose=True)
            if half == 0:
                v_y2t_h1 = s_tp.v
        v_y2t = s_tp.v
        for l in range(NT):
            for hf in range(2):
                bk = 2 * (l % 2) + hf
                bank = PS[bk]
                w0 = p.use(f"ps{bk}") + [(s_tp, v_y2t_h1 if l < 8 else v_y2t),
                                         (s_wl[0], vl_wo)]
                for c in range(ND):
                    mm(bank[:, :], SCA[:, c * S + 128 * l: c * S + 128 * l + 128],
                       WB0[:, c * D + 512 * hf: c * D + 512 * hf + 512],
                       start=(c == 0), stop=(c == ND - 1), wait=(w0 if c == 0 else ()))
                vm = s_mm.v
                dst = R[:, l * D + 512 * hf: l * D + 512 * hf + 512]
                v_op("tensor_tensor", dst, bank[:, :], dst, ALU.add, wait=[(s_mm, vm)])
                if hf == 1:
                    p.release(f"rt{l}", s_dv)
                p.release(f"ps{bk}", s_dv)
        p.release("wb0", s_mm)
        p.drain("vector")

    def phase_moe(v_xnt, v_gain, v_lodump):
        vmm0 = s_mm.v
        v_op("memset", APx(PR, 7, [[128, 128], [8, 16], [1, 1]]), -1e30,
             wait=[(s_ac, s_ac.v)])
        v_prpad = s_dv.v
        def rchunk_dma(c):
            hslot = SCB[:, (c % 2) * 4096:(c % 2) * 4096 + 2048]
            lslot = SCB[:, (c % 2) * 4096 + 2048:(c % 2) * 4096 + 4096]
            w = p.use(f"rch{c % 2}")
            if c == 0:
                w = w + [(s_pp[0], v_lodump[0]), (s_pp[1], v_lodump[1]),
                         (s_pp[2], v_lodump[2]), (s_pp[3], v_lodump[3]),
                         (s_mm, vmm0), (s_dv, s_dv.v)]
            sem = s_pp[2 + c % 2]
            p.dma("sync", hslot, APx(d_xn, c * 128, [[D, S], [1, 128]]),
                  sem, wait=w, transpose=True)
            vl = p.dma("sync", lslot, APx(d_lo, c * 128, [[D, S], [1, 128]]),
                  sem, transpose=True)
            return (hslot, lslot, (sem, vl))

        rslots = {0: rchunk_dma(0), 1: rchunk_dma(1)}
        vl_g = cast_load(WB0[:, :], wg_s, D, D, s_wl[0], wait=p.use("wb0"))
        vl_u = cast_load(WB1[:, :], wu_s, D, D, s_wl[1], wait=p.use("wb1"))
        vl_d = cast_load(WB2[:, :], wd_s, D, D, s_wl[2], wait=p.use("wb2"))
        for ei in range(ER + 1):
            e = ei - 1
            if ei == 1:
                for c in range(ND):
                    hslot, lslot, vl = rslots[c]
                    for b in range(4):
                        bk = 4 + b
                        w0 = [vl] if b == 0 else []
                        if c == 0:
                            w0 = w0 + p.use(f"ps{bk}")
                        mm(PS[bk][0:7, :], WROUT[:, ER * c:ER * c + ER],
                           hslot[:, 512 * b:512 * b + 512],
                           start=(c == 0), stop=False, wait=w0)
                        mm(PS[bk][0:7, :], WRL[:, ER * c:ER * c + ER],
                           hslot[:, 512 * b:512 * b + 512], start=False, stop=False)
                        mm(PS[bk][0:7, :], WROUT[:, ER * c:ER * c + ER],
                           lslot[:, 512 * b:512 * b + 512], start=False,
                           stop=(c == ND - 1))
                    p.release(f"rch{c % 2}", s_mm)
                    if c + 2 < ND:
                        rslots[c + 2] = rchunk_dma(c + 2)
                vmm_rt = s_mm.v
                for b in range(4):
                    bk = 4 + b
                    for j in range(4):
                        l = 4 * b + j
                        v_op("tensor_copy", RLG[0:7, :], PS[bk][0:7, 128 * j:128 * j + 128],
                             wait=[(s_mm, vmm_rt)] + p.use("rlg"))
                        p.drain("vector")
                        vr = s_dv.v
                        w7 = p.use("ps3")

                        def tb(en):
                            return en.transpose(PS[3][:, 0:7], RLG[0:7, :], IDENT[0:7, 0:7])
                        p.on("tensor", tb, wait=w7 + [(s_dv, vr)], inc=s_mm, n=1)
                        p.release("rlg", s_mm)
                        vm = s_mm.v
                        v_op("tensor_tensor", SM0[:, 0:ER], PS[3][:, 0:ER], RBR[:, :], ALU.add,
                             wait=[(s_mm, vm), (s_ac, s_ac.v)])
                        p.drain("vector")
                        p.release("ps3", s_dv)
                        v_op("tensor_copy", PR[:, 8 * l:8 * l + ER], SM0[:, 0:ER])
                        p.drain("vector")
                    p.release(f"ps{bk}", s_dv)
                v_probs = s_dv.v

                for l in range(NT):
                    blk = PR[:, 8 * l:8 * l + 8]
                    w0 = [(s_dv, v_probs), (s_dv, v_prpad)] if l == 0 else []
                    v_op("max", out=SM0[:, 0:8], in_=blk, wait=w0)
                    p.drain("vector")
                    va0 = s_ac.v
                    a_op("activation", out=SM0[:, 8:10], in_=SM0[:, 0:2], func=AF.Sigmoid,
                         wait=[(s_dv, s_dv.v)])
                    va = s_ac.v
                    v_op("tensor_tensor", SM0[:, 10:11], SM0[:, 8:9], SM0[:, 9:10], ALU.add,
                         wait=[(s_ac, va)])
                    p.drain("vector")
                    v_op("reciprocal", SM0[:, 11:12], SM0[:, 10:11])
                    p.drain("vector")
                    v_op("tensor_mul", SM0[:, 12:13], SM0[:, 8:9], SM0[:, 11:12])
                    v_op("tensor_mul", SM0[:, 13:14], SM0[:, 9:10], SM0[:, 11:12])
                    p.drain("vector")
                    v_op("tensor_scalar", WTS[:, 8 * l:8 * l + ER], PR[:, 8 * l:8 * l + ER],
                         SM0[:, 0:1], SM0[:, 12:13], ALU.is_equal, ALU.mult)
                    p.drain("vector")
                    v_op("tensor_scalar", STAT[:, 128:128 + ER], PR[:, 8 * l:8 * l + ER],
                         SM0[:, 1:2], SM0[:, 13:14], ALU.is_equal, ALU.mult)
                    p.drain("vector")
                    v_op("tensor_tensor", WTS[:, 8 * l:8 * l + ER], WTS[:, 8 * l:8 * l + ER],
                         STAT[:, 128:128 + ER], ALU.add)
                    p.drain("vector")
                v_wts = s_dv.v

            def emit_up_evacs(pic, plh, pbks, pvm):
                for lb2 in range(2):
                    lb = 2 * plh + lb2
                    a_op("activation", out=SCA[:, 16384:16896],
                         in_=PS[pbks[lb2]][:, :], func=AF.Silu,
                         wait=[(s_mm, pvm)] + p.use("silu_s"))
                    va_si = s_ac.v
                    p.release(f"ps{pbks[lb2]}", s_ac)
                    v_op("tensor_tensor",
                         SCA[:, 2048 * pic + 512 * lb:2048 * pic + 512 * lb + 512],
                         SCA[:, 16384:16896], PS[pbks[2 + lb2]][:, :], ALU.mult,
                         wait=[(s_mm, pvm), (s_ac, va_si)] + p.use("silu_s"))
                    p.drain("vector")
                    p.release("silu_s", s_dv)
                    p.release(f"ps{pbks[2 + lb2]}", s_dv)

            pend = None
            for ic in range(ND):
                for lh in range(2):
                    t = 2 * ic + lh
                    bks = (0, 1, 2, 3) if t % 2 == 0 else (4, 5, 6, 7)
                    for (wslot, swl, vw, bsub) in ((WB0, s_wl[0], vl_g, (0, 1)),
                                                   (WB1, s_wl[1], vl_u, (2, 3))):
                        for lb2 in range(2):
                            bk = bks[bsub[lb2]]
                            bank = PS[bk]
                            lb = 2 * lh + lb2
                            w0 = p.use(f"ps{bk}") + [(s_dv, v_xnt), (s_dv, v_gain),
                                                     (swl, vw)]
                            for c in range(ND):
                                mm(bank[:, :],
                                   wslot[:, c * D + 128 * ic: c * D + 128 * ic + 128],
                                   XNT[:, c * S + 512 * lb: c * S + 512 * lb + 512],
                                   start=(c == 0), stop=(c == ND - 1),
                                   wait=(w0 if c == 0 else ()))
                    vm_up = s_mm.v
                    if pend is not None:
                        emit_up_evacs(*pend)
                    pend = (ic, lh, bks, vm_up)
            emit_up_evacs(*pend)
            v_hp = s_dv.v
            p.release("wb0", s_mm)
            p.release("wb1", s_mm)
            if ei < ER:
                vl_g = cast_load(WB0[:, :], wg_r, D, D, s_wl[0],
                                 wait=p.use("wb0"), row_off=ei * D)
                vl_u = cast_load(WB1[:, :], wu_r, D, D, s_wl[1],
                                 wait=p.use("wb1"), row_off=ei * D)
            for lc in range(NT):
                bks = (0, 1) if lc % 2 == 0 else (2, 3)
                for ic in range(ND):
                    for hf in range(2):
                        bk = bks[hf]
                        w0 = (p.use(f"ps{bk}") + [(s_dv, v_hp), (s_wl[2], vl_d)]
                              if ic == 0 else ())
                        mm(PS[bk][:, :],
                           SCA[:, 2048 * ic + 128 * lc: 2048 * ic + 128 * lc + 128],
                           WB2[:, ic * D + 512 * hf: ic * D + 512 * hf + 512],
                           start=(ic == 0), stop=(ic == ND - 1), wait=w0)
                vm = s_mm.v
                for hf in range(2):
                    bk = bks[hf]
                    dst = R[:, lc * D + 512 * hf: lc * D + 512 * hf + 512]
                    if e >= 0:
                        w0 = [(s_mm, vm)] + ([(s_dv, v_wts)]
                                             if ei == 1 and lc == 0 else [])
                        v_op("scalar_tensor_tensor", dst, PS[bk][:, :],
                             WTS[:, 8 * lc + e:8 * lc + e + 1], dst,
                             ALU.mult, ALU.add, wait=w0)
                    else:
                        v_op("tensor_tensor", dst, PS[bk][:, :], dst, ALU.add,
                             wait=[(s_mm, vm)])
                    if hf == 1 and ei == ER:
                        p.release(f"rt{lc}", s_dv)
                    p.release(f"ps{bk}", s_dv)
            p.release("wb2", s_mm)
            if ei < ER:
                vl_d = cast_load(WB2[:, :], wd_r, D, D, s_wl[2],
                                 wait=p.use("wb2"), row_off=ei * D)
        p.drain("vector")

    def phase_ffn(call):
        vst = norm_stats("ln", use_rt=True, fuse_rms=True)
        norm_apply_fused(vst, spill=True)
        v_gain = xnt_gain(WLN1)
        v_xnt = v_gain
        if stop_after is not None and stop_after == ("f_ln1" if call == 0 else "zz"):
            dump_dbg()
            return True
        if phase_mla(v_xnt, v_gain):
            return True
        if stop_after is not None and stop_after == ("mla1" if call == 0 else "zz"):
            dump_dbg()
            return True
        vst3 = norm_stats("rms", use_rt=True)
        ppv3 = norm_apply_dump("rms", vst3, dump=True, lo_dump=True)
        v_gain2 = xnt_gain(WLN2)
        phase_moe(v_gain2, v_gain2, ppv3)
        if stop_after is not None and stop_after == ("moe1" if call == 0 else "zz"):
            dump_dbg()
            return True
        for g_ in range(4):
            wg_ = [(s_mm, s_mm.v), (s_ac, s_ac.v)] if g_ == 0 else []
            for j_ in range(4):
                wg_ = wg_ + p.use(f"rt{4 * g_ + j_}")
            src = APx(sp_x, g_ * 4 * 128 * D, [[D, 128], [128 * D, 4], [1, D]])
            dst = R[:, g_ * 4 * D:(g_ + 1) * 4 * D]

            def bb(e, dst=dst, src=src):
                return e.dma_start(out=dst, in_=src, accum_op=ALU.add)
            vg = p.on("gpsimd", bb, wait=wg_, inc=s_rl[g_], n=16)
            for j_ in range(4):
                p.rel[f"rt{4 * g_ + j_}"] = (s_rl[g_], vg)
        if stop_after is not None and stop_after == ("ffn1" if call == 0 else "ffn2"):
            dump_dbg()
            return True
        return False

    def phase_attn_b():
        vst = norm_stats("ln", use_rt=True)
        v_xnt = norm_apply_dump("ln", vst)[4]
        vmm0 = s_mm.v
        vl_q = cast_load(WB0[:, :], wqb, D, D, s_wl[0], wait=p.use("wb0"))
        vl_k = cast_load(WB1[:, :], wkb, D, D, s_wl[1], wait=p.use("wb1"))
        vl_v = cast_load(WB2[:, :], wvb, D, D, s_wl[2], wait=p.use("wb2"))

        for c2 in range(ND):
            for lb in range(4):
                bk = lb % 2
                bank = PS[bk]
                w0 = p.use(f"ps{bk}") + [(s_dv, v_xnt), (s_wl[2], vl_v)]
                for c in range(ND):
                    mm(bank[:, :], WB2[:, c * D + 128 * c2: c * D + 128 * c2 + 128],
                       XNT[:, c * S + 512 * lb: c * S + 512 * lb + 512],
                       start=(c == 0), stop=(c == ND - 1), wait=(w0 if c == 0 else ()))
                vm = s_mm.v
                v_op("tensor_copy", SCA[:, 2048 * c2 + 512 * lb:2048 * c2 + 512 * lb + 512],
                     bank[:, :], wait=[(s_mm, vm)])
                p.release(f"ps{bk}", s_dv)
        p.release("wb2", s_mm)
        p.drain("vector")
        v_vt = s_dv.v

        for l in range(NT):
            for (wslot, swl, vw, bks, dsto) in ((WB0, s_wl[0], vl_q, (0, 1), 0),
                                                (WB1, s_wl[1], vl_k, (2, 3), 2048)):
                for hf in range(2):
                    bk = bks[hf]
                    bank = PS[bk]
                    w0 = p.use(f"ps{bk}") + [(s_dv, v_xnt), (swl, vw)]
                    for c in range(ND):
                        mm(bank[:, :],
                           XNT[:, c * S + 128 * l: c * S + 128 * l + 128],
                           wslot[:, c * D + 512 * hf: c * D + 512 * hf + 512],
                           start=(c == 0), stop=(c == ND - 1),
                           wait=(w0 if c == 0 else ()))
                    vm = s_mm.v
                    v_op("tensor_copy", SCB[:, dsto + 1024 * (l % 2) + 512 * hf:
                                            dsto + 1024 * (l % 2) + 512 * hf + 512],
                         bank[:, :], wait=[(s_mm, vm)])
                    p.release(f"ps{bk}", s_dv)
            p.drain("vector")
            v_qkt = s_dv.v
            w4 = p.use("ps4")
            for h_ in range(H):
                qsl = SCB[:, 1024 * (l % 2) + 64 * h_:1024 * (l % 2) + 64 * h_ + 64]
                ksl = SCB[:, 2048 + 1024 * (l % 2) + 64 * h_:
                          2048 + 1024 * (l % 2) + 64 * h_ + 64]
                po = 64 * (h_ % 2)
                mm(PS[4][po:po + 64, 64 * (h_ // 2):64 * (h_ // 2) + 64],
                   ksl, qsl, start=True, stop=True,
                   wait=(w4 + [(s_dv, v_qkt)] if h_ == 0 else ()))
            vm = s_mm.v
            v_op("tensor_copy", SCB[:, 4096:4608], PS[4][:, :], wait=[(s_mm, vm)])
            p.drain("vector")
            p.release("ps4", s_dv)
            w6 = p.use("ps6")
            if l == 0:
                v_op("tensor_copy", PS[6][:, :], SCB[:, 4096:4608], wait=w6)
            else:
                v_op("tensor_tensor", PS[6][:, :], SCB[:, 4096:4608], PS[6][:, :],
                     ALU.add, wait=w6)
            p.drain("vector")
            p.release("ps6", s_dv)
        v_z1 = s_dv.v
        p.release("wb0", s_mm)
        p.release("wb1", s_mm)
        vl_wo = cast_load(WB0[:, :], wob, D, D, s_wl[0], wait=p.use("wb0"))

        for c in range(8):
            zsl = PS[6][:, 64 * c:64 * c + 64]
            v_op("reduce_max", SM0[:, 0:1], zsl, axis=AX,
                 wait=([(s_dv, v_z1)] if c == 0 else ()))
            p.drain("vector")
            v_op("tensor_scalar_mul", SM0[:, 1:2], SM0[:, 0:1], -MLA_SC)
            p.drain("vector")
            vv = s_dv.v
            a_op("activation", out=SCB[:, 4608 + 64 * c:4672 + 64 * c],
                 in_=zsl, func=AF.Exp, scale=MLA_SC, bias=SM0[:, 1:2],
                 accum_out=SM0[:, 2:3], wait=[(s_dv, vv)])
            va = s_ac.v
            v_op("reciprocal", SM0[:, 3:4], SM0[:, 2:3], wait=[(s_ac, va)])
            p.drain("vector")
            v_op("tensor_scalar_mul", SCA[:, 2048 * c:2048 * c + 2048],
                 SCA[:, 2048 * c:2048 * c + 2048], SM0[:, 3:4])
            p.drain("vector")
        v_sm = s_dv.v

        vmm_z1done = s_mm.v
        for h_ in range(H):
            po = 64 * (h_ % 2)
            psl = slice(po, po + 64)
            for lb in range(4):
                bk = 2 + (lb % 2)
                bank = PS[bk]
                w0 = p.use(f"ps{bk}") + [(s_dv, v_sm)]
                mm(bank[po:po + 64, :],
                   SCB[psl, 4608 + 64 * (h_ // 2):4672 + 64 * (h_ // 2)],
                   SCA[psl, 2048 * (h_ // 2) + 512 * lb:2048 * (h_ // 2) + 512 * lb + 512],
                   start=True, stop=True, wait=w0)
                vm = s_mm.v
                v_op("tensor_copy",
                     XNT[psl, 2048 * (h_ // 2) + 512 * lb:2048 * (h_ // 2) + 512 * lb + 512],
                     bank[po:po + 64, :], wait=[(s_mm, vm)])
                p.release(f"ps{bk}", s_dv)
        p.drain("vector")
        v_z2 = s_dv.v

        for l in range(NT):
            for hf in range(2):
                bk = 4 + 2 * (l % 2) + hf
                bank = PS[bk]
                w0 = p.use(f"ps{bk}") + [(s_dv, v_z2), (s_wl[0], vl_wo)]
                for c in range(ND):
                    mm(bank[:, :], XNT[:, c * S + 128 * l: c * S + 128 * l + 128],
                       WB0[:, c * D + 512 * hf: c * D + 512 * hf + 512],
                       start=(c == 0), stop=(c == ND - 1), wait=(w0 if c == 0 else ()))
                vm = s_mm.v
                dst = R[:, l * D + 512 * hf: l * D + 512 * hf + 512]
                v_op("tensor_tensor", dst, bank[:, :], dst, ALU.add, wait=[(s_mm, vm)])
                if hf == 1:
                    p.release(f"rt{l}", s_dv)
                p.release(f"ps{bk}", s_dv)
        p.release("wb0", s_mm)
        p.drain("vector")

    v_wq0 = cast_load(WB0[:, :], wqa, D, D, s_wl[0])
    v_wk0 = cast_load(WB1[:, :], wka, D, D, s_wl[1])
    v_wv0 = cast_load(WB2[:, :], wva, D, D, s_wl[2])

    done = False
    phase_attn_a(v_wq0, v_wk0, v_wv0)
    if stop_after == "attn_a":
        dump_dbg()
        done = True
    if not done:
        done = phase_ffn(0)
    if not done and stop_after == "attn_b_pre":
        dump_dbg()
        done = True
    if not done:
        phase_attn_b()
        if stop_after == "attn_b":
            dump_dbg()
            done = True
    if not done:
        done = phase_ffn(1)

    wout = [(s_dv, s_dv.v), (s_ac, s_ac.v), (s_mm, s_mm.v), (s_gld, s_gld.v)]
    for i_ in range(NT):
        wout += p.use(f"rt{i_}")
    big_wdma("sync", out_t, R, s_st, wait=wout)
    p.wait("gpsimd", (s_st, s_st.v))
    p.finalize()
    return nc, di


def _rope_tables():
    inv = 1.0 / (10000.0 ** (np.arange(0, HH, 2, dtype=np.float64) / HH))
    t = np.arange(S, dtype=np.float64)
    freqs = np.einsum("i,j->ij", t, inv)
    emb = np.concatenate([freqs, freqs], -1)
    cos = np.cos(emb).T
    sin = np.sin(emb).T
    sinm = np.concatenate([-sin[0:16], sin[16:32]], 0)
    cos2 = np.tile(cos, (4, 1)).astype(np.float32)
    sinm2 = np.tile(sinm, (4, 1)).astype(np.float32)
    cos2[0:32] = 1.0
    cos2[64:96] = 1.0
    sinm2[0:32] = 0.0
    sinm2[64:96] = 0.0
    return (np.tile(cos, (4, 1)).astype(np.float32),
            np.tile(sinm, (4, 1)).astype(np.float32), cos2, sinm2)


def _host_inputs(inputs, core):
    import ml_dtypes
    z = inputs
    bf = ml_dtypes.bfloat16

    def T(a):
        return np.ascontiguousarray(np.asarray(a, dtype=np.float32).T)

    def C(a):
        return np.ascontiguousarray(np.asarray(a, dtype=np.float32))

    def fcols(b):
        return np.ascontiguousarray(np.asarray(b, np.float32).reshape(ND, 128).T)

    cos, sinm, cos2, sinm2 = _rope_tables()
    constf = np.zeros((128, 16), np.float32)
    constf[:, 0] = 1.0
    constf[:, 4] = LN_EPS
    constf[:, 5] = math.log(EPS_A)
    constb = np.zeros((128, 514), np.float32)
    pp_, jj = np.mgrid[0:128, 0:128]
    constb[:, 0:128] = (pp_ <= jj).astype(np.float32)
    constb[:, 128] = 1.0
    constb[:, 129] = -1.0
    constb[0:64, 130:194] = 1.0
    constb[64:128, 194:258] = 1.0
    constb[0:64, 258:386] = -1.0 / 128.0
    constb[64:128, 386:514] = -1.0 / 128.0
    wku_ = np.asarray(z["L0_ffn_W_ku"], np.float32)
    wqu_ = np.asarray(z["L0_ffn_W_qu"], np.float32)
    wrq_ = np.asarray(z["L0_ffn_W_ropeq"], np.float32)
    wku_p_ = np.zeros((LAT, 1024), np.float32)
    wq_c_ = np.zeros((LAT, 1024), np.float32)
    for h_ in range(H):
        base = 128 * (h_ // 2) + 64 * (h_ % 2)
        wku_p_[:, base:base + 32] = wku_[:, 32 * h_:32 * h_ + 32]
        wq_c_[:, base:base + 32] = wqu_[:, 32 * h_:32 * h_ + 32]
        wq_c_[:, base + 32:base + 64] = wrq_[:, 32 * h_:32 * h_ + 32]
    crow_ = np.zeros((1, 2048), np.float32)
    crow_[0, 0:1024] = np.asarray(z["L0_attn_a_bv"], np.float32)
    crow_[0, 1024:2048] = np.asarray(z["L0_attn_a_bo"], np.float32)
    return {
        "x": np.ascontiguousarray(np.asarray(z["L0_attn_a_x"])[core], dtype=np.float32),
        "wqa": T(z["L0_attn_a_Wq"]), "wka": T(z["L0_attn_a_Wk"]),
        "wva": T(z["L0_attn_a_Wv"]), "woa": T(z["L0_attn_a_Wo"]),
        "bqa": fcols(z["L0_attn_a_bq"]), "bka": fcols(z["L0_attn_a_bk"]),
        "orfT": np.ascontiguousarray(np.tile(np.asarray(z["L0_attn_a_orf"],
                                                        np.float32).T, (2, 1))),
        "wqb": T(z["L0_attn_b_Wq"]), "wkb": T(z["L0_attn_b_Wk"]),
        "wvb": T(z["L0_attn_b_Wv"]), "wob": T(z["L0_attn_b_Wo"]),
        "wkvd": C(z["L0_ffn_W_kvd"]), "wqd": C(z["L0_ffn_W_qd"]),
        "wku_p": wku_p_, "wq_c": wq_c_,
        "wvu": C(z["L0_ffn_W_vu"]),
        "wropek": C(z["L0_ffn_W_ropek"]),
        "wo_m": C(z["L0_ffn_W_o"]),
        "w_ln1f": fcols(z["L0_ffn_w_ln1"]), "w_ln2f": fcols(z["L0_ffn_w_ln2"]),
        "wg_s": C(z["L0_ffn_Wg_s"]), "wu_s": C(z["L0_ffn_Wu_s"]),
        "wd_s": C(z["L0_ffn_Wd_s"]),
        "wg_r": C(z["L0_ffn_Wg_r"]), "wu_r": C(z["L0_ffn_Wu_r"]),
        "wd_r": C(z["L0_ffn_Wd_r"]),
        "wrout": C(z["L0_ffn_W_router"]),
        "rb_row": np.ascontiguousarray(np.broadcast_to(
            np.asarray(z["L0_ffn_routing_bias"], np.float32), (128, ER))),
        "identf": np.eye(128, dtype=np.float32),
        "identb": np.eye(128, dtype=np.float32).astype(bf),
        "constf": constf,
        "constb": constb.astype(bf),
        "crow": crow_.astype(bf),
        "ropec": cos.astype(bf),
        "ropes": sinm.astype(bf),
        "ropec2": cos2.astype(bf),
        "ropes2": sinm2.astype(bf),
    }


def kernel(**inputs):
    nc, _ = build_nc()
    in_maps = [_host_inputs(inputs, core) for core in range(8)]
    res = run_bass_kernel_spmd(nc, in_maps, core_ids=list(range(8)))
    return np.stack([np.asarray(r["out"], np.float32) for r in res.results], axis=0)


# version 17 (final)
# speedup vs baseline: 1.0061x; 1.0061x over previous
import math
import numpy as np
import concourse.bass as bass
import concourse.mybir as mybir
from concourse.bass_utils import run_bass_kernel_spmd

F32 = mybir.dt.float32
BF16 = mybir.dt.bfloat16
AF = mybir.ActivationFunctionType
ALU = mybir.AluOpType
AX = mybir.AxisListType.X

S, D, NT, ND = 2048, 1024, 16, 8
H, DK, MF, LAT, HH, ER = 16, 64, 256, 256, 32, 7
LN_EPS = 1e-5
RMS_EPS = 1e-5
EPS_A = 1e-6
S1A = 1.0 / (DK ** 0.25)
S2A = 1.0 / (2.0 * math.sqrt(DK))
EPSP = EPS_A / math.sqrt(MF)
MLA_SC = 1.0 / 8.0
SQ8 = 2.0 ** -1.5

ENGS = ("sync", "tensor", "vector", "scalar", "gpsimd")


class Sem:
    __slots__ = ("h", "v", "name")

    def __init__(self, h, name):
        self.h = h
        self.v = 0
        self.name = name


class Prog:
    def __init__(self, nc):
        self.nc = nc
        self.q = {e: [] for e in ENGS}
        self._ctx = []
        self.rel = {}

    def sem(self, name):
        g = self.nc.semaphore(name)
        h = g.__enter__()
        self._ctx.append(g)
        return Sem(h, name)

    def on(self, eng, build, wait=(), inc=None, n=1):
        w = tuple((s, v) for (s, v) in wait if v > 0)
        if inc is not None:
            inc.v += n

        def f(e, build=build, w=w, inc=inc, n=n):
            for s, v in w:
                e.wait_ge(s.h, v)
            ins = build(e)
            if inc is not None:
                ins.then_inc(inc.h, n)
        self.q[eng].append(f)
        return inc.v if inc is not None else None

    def wait(self, eng, *pairs):
        ps = tuple(pairs)

        def f(e, ps=ps):
            for s, v in ps:
                if v > 0:
                    e.wait_ge(s.h, v)
        self.q[eng].append(f)

    def drain(self, eng):
        self.q[eng].append(lambda e: e.drain())

    def dma(self, eng, out, in_, sem, wait=(), transpose=False):
        def b(e, out=out, in_=in_, transpose=transpose):
            if transpose:
                return e.dma_start(out=out, in_=in_, transpose=True)
            return e.dma_start(out=out, in_=in_)
        return self.on(eng, b, wait=wait, inc=sem, n=16)

    def release(self, res, sem):
        self.rel[res] = (sem, sem.v)

    def use(self, res):
        r = self.rel.get(res)
        return [r] if r is not None else []

    def finalize(self):
        with self.nc.Block() as block:
            for name in ENGS:
                ops = self.q[name]

                def body(e, ops=ops):
                    for f_ in ops:
                        f_(e)
                getattr(block, name)(body)


def APx(t, off, dims):
    return bass.AP(t, off, [list(d) for d in dims])


def build_nc(stop_after=None):
    nc = bass.Bass("TRN2", target_bir_lowering=False)
    di = {}

    def dram_in(name, shape, dt=F32):
        di[name] = nc.dram_tensor(name, list(shape), dt, kind="ExternalInput")
        return di[name]

    x_in = dram_in("x", [S, D])
    wqa = dram_in("wqa", [D, D]); wka = dram_in("wka", [D, D])
    wva = dram_in("wva", [D, D]); woa = dram_in("woa", [D, D])
    bqa_d = dram_in("bqa", [128, ND]); bka_d = dram_in("bka", [128, ND])
    orft_d = dram_in("orfT", [128, MF])
    wqb = dram_in("wqb", [D, D]); wkb = dram_in("wkb", [D, D])
    wvb = dram_in("wvb", [D, D]); wob = dram_in("wob", [D, D])
    wkvd = dram_in("wkvd", [D, LAT]); wqd = dram_in("wqd", [D, LAT])
    wku_p = dram_in("wku_p", [LAT, 1024]); wq_c = dram_in("wq_c", [LAT, 1024])
    wvu = dram_in("wvu", [LAT, D])
    wropek = dram_in("wropek", [D, 512])
    wo_m = dram_in("wo_m", [D, D])
    w_ln1f = dram_in("w_ln1f", [128, ND]); w_ln2f = dram_in("w_ln2f", [128, ND])
    wg_s = dram_in("wg_s", [D, D]); wu_s = dram_in("wu_s", [D, D]); wd_s = dram_in("wd_s", [D, D])
    wg_r = dram_in("wg_r", [ER, D, D]); wu_r = dram_in("wu_r", [ER, D, D])
    wd_r = dram_in("wd_r", [ER, D, D])
    wrout_d = dram_in("wrout", [D, ER])
    rb_d = dram_in("rb_row", [128, ER])
    identf = dram_in("identf", [128, 128])
    identb_d = dram_in("identb", [128, 128], BF16)
    constf = dram_in("constf", [128, 16])
    constb_d = dram_in("constb", [128, 514], BF16)
    crow_d = dram_in("crow", [1, 2048])
    ropec = dram_in("ropec", [128, S])
    ropes = dram_in("ropes", [128, S])
    ropec2 = dram_in("ropec2", [128, S])
    ropes2 = dram_in("ropes2", [128, S])

    out_t = nc.dram_tensor("out", [S, D], F32, kind="ExternalOutput")
    dbg_t = None
    if stop_after is not None:
        dbg_t = nc.dram_tensor("dbg", [S, D], F32, kind="ExternalOutput")

    sp_x = nc.dram_tensor("sp_x", [S, D], F32, kind="Internal")
    d_xn = nc.dram_tensor("d_xn", [S, D], BF16, kind="Internal")
    d_ya = nc.dram_tensor("d_ya", [S, D], BF16, kind="Internal")
    d_yraw = nc.dram_tensor("d_yraw", [1040, S], BF16, kind="Internal")
    d_y2 = nc.dram_tensor("d_y2", [S, D], BF16, kind="Internal")
    d_lo = nc.dram_tensor("d_lo", [S, D], BF16, kind="Internal")

    ctx = []

    def sbt(name, shape, dt=F32):
        g = nc.sbuf_tensor(name, list(shape), dt)
        t = g.__enter__()
        ctx.append(g)
        return t

    def psb(name, shape):
        g = nc.psum_tensor(name, list(shape), F32)
        t = g.__enter__()
        ctx.append(g)
        return t

    R = sbt("R", [128, NT * D])
    XNT = sbt("XNT", [128, ND * S], BF16)
    WB0 = sbt("WB0", [128, 8 * D], BF16)
    WB1 = sbt("WB1", [128, 8 * D], BF16)
    WB2 = sbt("WB2", [128, 8 * D], BF16)
    SCA = sbt("SCA", [128, 17408], BF16)
    SCB = sbt("SCB", [128, 8192], BF16)
    SCF = sbt("SCF", [128, 2056], BF16)
    IDENT = sbt("IDENT", [128, 128], F32)
    IDENTB = sbt("IDENTB", [128, 128], BF16)
    CONST = sbt("CONST", [128, 16], F32)
    CONSTB = sbt("CONSTB", [128, 514], BF16)
    CROW = sbt("CROW", [1, 2048], BF16)
    BQA = sbt("BQA", [128, ND], F32)
    BKA = sbt("BKA", [128, ND], F32)
    WLN1 = sbt("WLN1", [128, ND], F32)
    WLN2 = sbt("WLN2", [128, ND], F32)
    RBR = sbt("RBR", [128, ER], F32)
    ORFT = sbt("ORFT", [128, MF], BF16)
    STAT = sbt("STAT", [128, 136], F32)
    SM0 = sbt("SM0", [128, 16], F32)
    WROUT = sbt("WROUT", [128, ND * ER], BF16)
    WRL = sbt("WRL", [128, ND * ER], BF16)
    RLG = sbt("RLG", [7, 128], F32)
    PR = sbt("PR", [128, 128], F32)
    WTS = STAT

    PS = [psb(f"PS{i}", [128, 512]) for i in range(8)]

    p = Prog(nc)
    s_ld = p.sem("s_ld")
    s_tp = p.sem("s_tp")
    s_st = p.sem("s_st")
    s_gld = p.sem("s_gld")
    s_mm = p.sem("s_mm")
    s_dv = p.sem("s_dv")
    s_ac = p.sem("s_ac")
    s_pp = [p.sem(f"s_pp{i}") for i in range(4)]
    s_wl = [p.sem(f"s_wl{i}") for i in range(3)]
    s_dbg = p.sem("s_dbg")
    s_rl = [p.sem(f"s_rl{i}") for i in range(4)]
    s_tb = p.sem("s_tb")
    s_sc = p.sem("s_sc")
    s_yd = p.sem("s_yd")
    s_y2 = p.sem("s_y2")

    def mm(out_ap, lhsT, rhs, start, stop, wait=(), tp=None):
        def b(e, out_ap=out_ap, lhsT=lhsT, rhs=rhs, start=start, stop=stop, tp=tp):
            return e.matmul(out_ap, lhsT, rhs, start=start, stop=stop, tile_position=tp)
        return p.on("tensor", b, wait=wait, inc=s_mm, n=1)

    def v_op(name, *a, wait=(), **kw):
        def b(e, name=name, a=a, kw=kw):
            return getattr(e, name)(*a, **kw)
        return p.on("vector", b, wait=wait, inc=s_dv, n=1)

    def a_op(name, *a, wait=(), **kw):
        def b(e, name=name, a=a, kw=kw):
            return getattr(e, name)(*a, **kw)
        return p.on("scalar", b, wait=wait, inc=s_ac, n=1)


    def big_rdma(eng, dram_t, sbuf_t, sem, wait=(), accum=False, ngrp=4):
        per = NT // ngrp
        vals = []
        for g_ in range(ngrp):
            src = APx(dram_t, g_ * per * 128 * D, [[D, 128], [128 * D, per], [1, D]])
            dst = sbuf_t[:, g_ * per * D:(g_ + 1) * per * D]
            if accum:
                def b(e, dst=dst, src=src):
                    return e.dma_start(out=dst, in_=src, accum_op=ALU.add)
                vals.append(p.on(eng, b, wait=(wait if g_ == 0 else ()), inc=sem, n=16))
            else:
                vals.append(p.dma(eng, dst, src, sem, wait=(wait if g_ == 0 else ())))
        return vals[-1]

    def big_wdma(eng, dram_t, sbuf_t, sem, wait=(), ngrp=4):
        per = NT // ngrp
        vals = []
        for g_ in range(ngrp):
            dst = APx(dram_t, g_ * per * 128 * D, [[D, 128], [128 * D, per], [1, D]])
            src = sbuf_t[:, g_ * per * D:(g_ + 1) * per * D]
            vals.append(p.dma(eng, dst, src, sem, wait=(wait if g_ == 0 else ())))
        return vals[-1]

    def ld2(dst_ap, src_t, rows, cols):
        return p.dma("sync", dst_ap, APx(src_t, 0, [[cols, rows], [1, cols]]), s_ld)

    ld2(IDENT[:, :], identf, 128, 128)
    ld2(IDENTB[:, :], identb_d, 128, 128)
    ld2(CONST[:, :], constf, 128, 16)
    ld2(CONSTB[:, :], constb_d, 128, 514)
    v_crow = p.dma("gpsimd", CROW[:, :], APx(crow_d, 0, [[2048, 1], [1, 2048]]),
                   s_gld)
    ld2(BQA[:, :], bqa_d, 128, ND)
    ld2(BKA[:, :], bka_d, 128, ND)
    ld2(WLN1[:, :], w_ln1f, 128, ND)
    ld2(WLN2[:, :], w_ln2f, 128, ND)
    ld2(RBR[:, :], rb_d, 128, ER)
    big_rdma("sync", x_in, R, s_ld)
    v_wrt = p.dma("sync", STAT[:, 0:ND * ER],
                  APx(wrout_d, 0, [[ER, 128], [128 * ER, ND], [1, ER]]), s_ld)
    v_preload = s_ld.v
    v_op("memset", SCA[:, 16704:17408], 0.0)
    p.drain("vector")
    v_orf0 = p.dma("gpsimd", ORFT[:, :], APx(orft_d, 0, [[MF, 128], [1, MF]]),
                   s_gld, wait=[(s_gld, v_crow)])
    v_op("tensor_scalar_mul", ORFT[:, :], ORFT[:, :], S1A, wait=[(s_gld, v_orf0)])
    p.drain("vector")
    for c in range(ND):
        v_op("tensor_scalar_mul", STAT[:, ER * c:ER * c + ER],
             STAT[:, ER * c:ER * c + ER], WLN2[:, c:c + 1],
             wait=([(s_ld, v_preload), (s_ld, v_wrt)] if c == 0 else ()))
    p.drain("vector")
    v_op("tensor_copy", WROUT[:, :], STAT[:, 0:ND * ER])
    p.drain("vector")
    v_op("tensor_sub", WRL[:, :], STAT[:, 0:ND * ER], WROUT[:, :])
    p.drain("vector")
    v_orf = s_dv.v

    def cast_load(dst_ap, src_t, rows, cols, sem, wait=(), row_off=0):
        nch = rows // 128
        if nch <= 4:
            src = APx(src_t, row_off * cols, [[cols, 128], [128 * cols, nch], [1, cols]])
            return p.dma("gpsimd", dst_ap, src, sem, wait=wait)
        half = nch // 2
        t0 = dst_ap.tensor if hasattr(dst_ap, "tensor") else None
        src1 = APx(src_t, row_off * cols, [[cols, 128], [128 * cols, half], [1, cols]])
        src2 = APx(src_t, (row_off + half * 128) * cols,
                   [[cols, 128], [128 * cols, nch - half], [1, cols]])
        d1 = bass.AP(dst_ap.tensor, dst_ap.offset,
                     [list(dst_ap.ap[0]), [1, half * cols]])
        d2 = bass.AP(dst_ap.tensor, dst_ap.offset + half * cols,
                     [list(dst_ap.ap[0]), [1, (nch - half) * cols]])
        p.dma("gpsimd", d1, src1, sem, wait=wait)
        return p.dma("gpsimd", d2, src2, sem)

    def norm_stats(kind, wait_first=(), use_rt=False, fuse_rms=False):
        if not use_rt:
            wait_first = list(wait_first) + [(s_dv, s_dv.v), (s_ac, s_ac.v),
                                            (s_mm, s_mm.v), (s_gld, s_gld.v)]
        for i in range(NT):
            xc = R[:, i * D:(i + 1) * D]
            if use_rt:
                w = p.use(f"rt{i}") + (list(wait_first) if i == 0 else [])
            else:
                w = list(wait_first) if i == 0 else []
            wslot = p.use(f"xds{i % 2}")
            a_op("activation", out=SCF[:, (i % 2) * D:(i % 2) * D + D], in_=xc,
                 func=AF.Square, accum_out=STAT[:, i:i + 1], wait=w + wslot)
            p.drain("scalar")
            if kind == "ln":
                v_op("reduce_sum", STAT[:, 16 + i:17 + i], xc, axis=AX, wait=w)
        vq = s_ac.v
        if kind == "ln":
            p.drain("vector")
            v_op("tensor_scalar_mul", STAT[:, 32:48], STAT[:, 16:32], 1.0 / D)
            p.drain("vector")
            v_op("tensor_mul", STAT[:, 64:80], STAT[:, 32:48], STAT[:, 32:48])
            p.drain("vector")
            v_op("tensor_scalar", STAT[:, 48:64], STAT[:, 0:16], 1.0 / D, None,
                 ALU.mult, wait=[(s_ac, vq)])
            p.drain("vector")
            v_op("tensor_sub", STAT[:, 48:64], STAT[:, 48:64], STAT[:, 64:80])
            vv = s_dv.v
            a_op("activation", out=STAT[:, 64:80], in_=STAT[:, 48:64], func=AF.Sqrt,
                 bias=CONST[:, 4:5], wait=[(s_dv, vv)])
            va = s_ac.v
            v_op("reciprocal", STAT[:, 80:96], STAT[:, 64:80], wait=[(s_ac, va)])
            if fuse_rms:
                v_op("tensor_scalar", STAT[:, 16:32], STAT[:, 48:64],
                     1.0 + RMS_EPS, RMS_EPS * LN_EPS, ALU.mult, ALU.add,
                     wait=[(s_dv, s_dv.v)])
                p.drain("vector")
                vdf = s_dv.v
                a_op("activation", out=STAT[:, 0:16], in_=STAT[:, 16:32],
                     func=AF.Sqrt, wait=[(s_dv, vdf)])
                vaf = s_ac.v
                v_op("reciprocal", STAT[:, 96:112], STAT[:, 0:16],
                     wait=[(s_ac, vaf)])
        else:
            a_op("activation", out=STAT[:, 64:80], in_=STAT[:, 0:16], func=AF.Sqrt,
                 scale=1.0 / D, wait=[(s_ac, vq)])
            va = s_ac.v
            v_op("tensor_scalar_add", STAT[:, 48:64], STAT[:, 64:80], RMS_EPS,
                 wait=[(s_ac, va)])
            p.drain("vector")
            v_op("reciprocal", STAT[:, 80:96], STAT[:, 48:64])
        p.drain("vector")
        return s_dv.v

    def norm_apply_fused(v_stats, spill):
        v_sp = big_wdma("sync", sp_x, R, s_st, wait=[(s_dv, v_stats)]) if spill else 0
        vsc = s_sc.v
        for i in range(NT):
            xc = R[:, i * D:(i + 1) * D]
            slot = SCF[:, (i % 2) * D:(i % 2) * D + D]
            w = [(s_dv, v_stats), (s_ac, s_ac.v)] + p.use(f"xds{i % 2}")
            if spill:
                w.append((s_st, v_sp))
            v_op("tensor_scalar", slot, xc, STAT[:, 32 + i:33 + i],
                 STAT[:, 96 + i:97 + i], ALU.subtract, ALU.mult, wait=w)
            v_slot = s_dv.v
            p.drain("vector")
            v_op("tensor_scalar", xc, xc, STAT[:, 32 + i:33 + i],
                 STAT[:, 80 + i:81 + i], ALU.subtract, ALU.mult)
            xnt_tile_transpose(i, v_slot,
                               w_extra=([(s_sc, vsc)] if i == 0 else ()))
            p.release(f"xds{i % 2}", s_mm)
        p.drain("vector")
        return s_dv.v

    def norm_apply_dump(kind, v_stats, spill=False, dump=False, lo_dump=False):
        vw_pre = [(s_ac, s_ac.v)]
        v_sp = 0
        vsc = s_sc.v
        if spill:
            v_sp = big_wdma("sync", sp_x, R, s_st, wait=[(s_dv, v_stats)])
        for i in range(NT):
            xc = R[:, i * D:(i + 1) * D]
            slot = SCF[:, (i % 2) * D:(i % 2) * D + D]
            w = [(s_dv, v_stats)] + p.use(f"xds{i % 2}") + vw_pre
            if spill:
                w.append((s_st, v_sp))
            if kind == "ln":
                v_op("tensor_scalar", slot, xc, STAT[:, 32 + i:33 + i],
                     STAT[:, 80 + i:81 + i], ALU.subtract, ALU.mult, wait=w)
            else:
                v_op("tensor_scalar_mul", slot, xc, STAT[:, 80 + i:81 + i], wait=w)
            v_slot = s_dv.v
            if lo_dump:
                loslot = SCB[:, (i % 2) * D:(i % 2) * D + D]
                wlo = p.use(f"lo{i % 2}")
                if i < 2:
                    wlo = wlo + p.use("ytk0") + p.use("ytk1")
                p.drain("vector")
                v_op("scalar_tensor_tensor", loslot, xc, STAT[:, 80 + i:81 + i],
                     slot, ALU.mult, ALU.subtract, wait=wlo)
            vm_tp = xnt_tile_transpose(i, v_slot,
                                       w_extra=([(s_sc, vsc)] if i == 0 else ()))
            if dump:
                vd = s_dv.v
                p.dma("sync", APx(d_xn, i * 128 * D, [[D, 128], [1, D]]), slot,
                      s_pp[i % 2], wait=[(s_dv, vd), (s_mm, vm_tp)])
                p.release(f"xds{i % 2}", s_pp[i % 2])
                if lo_dump:
                    p.dma("sync", APx(d_lo, i * 128 * D, [[D, 128], [1, D]]), loslot,
                          s_pp[2 + i % 2], wait=[(s_dv, vd)])
                    p.release(f"lo{i % 2}", s_pp[2 + i % 2])
            else:
                p.release(f"xds{i % 2}", s_mm)
        p.drain("vector")
        return (s_pp[0].v, s_pp[1].v, s_pp[2].v, s_pp[3].v, s_dv.v)

    def xnt_read(wait_pp, gain=None):
        vals = []
        for c in range(ND):
            src = APx(d_xn, c * 128, [[D, S], [1, 128]])
            w = [(s_pp[0], wait_pp[0]), (s_pp[1], wait_pp[1]),
                 (s_mm, s_mm.v)] if c == 0 else ()
            vals.append(p.dma("sync", XNT[:, c * S:(c + 1) * S], src, s_tp,
                              wait=w, transpose=True))
        v_all = vals[-1]
        if gain is not None:
            for c in range(ND):
                v_op("tensor_scalar_mul", XNT[:, c * S:(c + 1) * S],
                     XNT[:, c * S:(c + 1) * S], gain[:, c:c + 1],
                     wait=([(s_tp, v_all)] if c == 0 else ()))
            p.drain("vector")
        return v_all, s_dv.v

    def xnt_tile_transpose(i, v_slot, w_extra=()):
        slot = SCF[:, (i % 2) * D:(i % 2) * D + D]
        bks = (0, 1) if i % 2 == 0 else (2, 3)
        vals = []
        for half in range(2):
            bk = bks[half]
            w0 = p.use(f"ps{bk}") + [(s_dv, v_slot), (s_ld, v_preload)]
            for j in range(4):
                c = 4 * half + j
                mm(PS[bk][:, 128 * j:128 * j + 128],
                   slot[:, 128 * c:128 * c + 128], IDENTB[:, :],
                   start=True, stop=True, wait=(w0 if j == 0 else ()))
            vals.append(s_mm.v)
        for half in range(2):
            bk = bks[half]
            src = APx(PS[bk], 0, [[512, 128], [128, 4], [1, 128]])
            dst = APx(XNT, S * 4 * half + 128 * i, [[NT * D, 128], [S, 4], [1, 128]])
            v_op("tensor_copy", dst, src,
                 wait=[(s_mm, vals[half])] + list(w_extra if half == 0 else ()))
            p.release(f"ps{bk}", s_dv)
        return s_mm.v

    def xnt_gain(gain):
        for c in range(ND):
            v_op("tensor_scalar_mul", XNT[:, c * S:(c + 1) * S],
                 XNT[:, c * S:(c + 1) * S], gain[:, c:c + 1])
        p.drain("vector")
        return s_dv.v

    def dump_dbg():
        wdbg = [(s_dv, s_dv.v), (s_ac, s_ac.v), (s_mm, s_mm.v), (s_gld, s_gld.v)]
        for i_ in range(NT):
            wdbg += p.use(f"rt{i_}")
        big_wdma("sync", dbg_t, R, s_dbg, wait=wdbg)
        p.wait("gpsimd", (s_dbg, s_dbg.v))

    def phase_attn_a(v_wq, v_wk, v_wv):
        vst = norm_stats("ln", wait_first=[(s_ld, v_preload)])
        v_xnt = norm_apply_dump("ln", vst)[4]

        eqo, eko, kvo = 8320, 12416, 16512

        def build_v_half_a(half, wait_extra):
            v_op("memset", APx(SCA, 64, [[17408, 128], [520, NT], [65, 8], [1, 1]]),
                 1.0, wait=wait_extra)
            v_ones = s_dv.v
            for l in range(NT):
                bk = l % 2
                bank = PS[bk]
                w0 = p.use(f"ps{bk}") + [(s_dv, v_ones)]
                if half == 0 and l == 0:
                    w0 += [(s_dv, v_xnt), (s_wl[2], v_wv), (s_gld, v_crow)]
                for c in range(ND):
                    mm(bank[:, :], XNT[:, c * S + 128 * l: c * S + 128 * l + 128],
                       WB2[:, c * D + half * 512: c * D + half * 512 + 512],
                       start=(c == 0), stop=False, wait=(w0 if c == 0 else ()))
                mm(bank[:, :], CONSTB[0:1, 0:128],
                   CROW[0:1, half * 512: half * 512 + 512], start=False, stop=True)
                vm = s_mm.v
                dst = APx(SCA, 520 * l, [[17408, 128], [65, 8], [1, 64]])
                src = APx(PS[bk], 0, [[512, 128], [64, 8], [1, 64]])
                v_op("tensor_copy", dst, src, wait=[(s_mm, vm)])
                p.release(f"ps{bk}", s_dv)
            p.drain("vector")
            return s_dv.v

        v_vdone = 0
        v_wo = None
        for pp_i in range(ND):
            if pp_i == 0:
                v_vdone = build_v_half_a(0, [])
            elif pp_i == 4:
                v_vdone = build_v_half_a(1, [(s_mm, s_mm.v)])
                v_wo = cast_load(WB2[:, :], woa, D, D, s_wl[2], wait=[(s_mm, s_mm.v)])

            for (wslot, swl, vw, bias, dsto) in ((WB0, s_wl[0], v_wq, BQA, 0),
                                                 (WB1, s_wl[1], v_wk, BKA, 2048)):
                for lb in range(4):
                    bk = 2 + (lb % 2)
                    bank = PS[bk]
                    wv = p.use(f"ps{bk}") + [(s_dv, v_xnt), (swl, vw)]
                    for c in range(ND):
                        mm(bank[:, :],
                           wslot[:, c * D + 128 * pp_i: c * D + 128 * pp_i + 128],
                           XNT[:, c * S + 512 * lb: c * S + 512 * lb + 512],
                           start=(c == 0), stop=(c == ND - 1),
                           wait=(wv if c == 0 else ()))
                    vm = s_mm.v
                    v_op("tensor_scalar_add",
                         SCB[:, dsto + 512 * lb: dsto + 512 * lb + 512],
                         bank[:, :], bias[:, pp_i:pp_i + 1], wait=[(s_mm, vm)])
                    p.release(f"ps{bk}", s_dv)
            if pp_i == ND - 1:
                p.release("wb0", s_mm)
                p.release("wb1", s_mm)
            p.drain("vector")
            v_qk = s_dv.v

            for hh in range(2):
                gh = 2 * pp_i + hh
                hloc = gh % 8
                hsl = slice(64 * hh, 64 * hh + 64)
                onescol = CONSTB[hsl, 128:129]
                v_op("tensor_mul", SCB[:, 4096:6144], SCB[:, 2048:4096],
                     SCB[:, 2048:4096], wait=[(s_dv, v_qk), (s_mm, s_mm.v)])
                p.drain("vector")
                v_sk = s_dv.v
                w4 = p.use("ps4")
                for l in range(NT):
                    mm(PS[4][:, l:l + 1], SCB[hsl, 4096 + 128 * l:4224 + 128 * l],
                       onescol, start=True, stop=True,
                       wait=(w4 + [(s_dv, v_sk)] if l == 0 else ()))
                vm_nk = s_mm.v
                v_op("tensor_scalar", STAT[:, 0:16], PS[4][:, 0:16], -S2A,
                     -math.log(MF) / 2.0, ALU.mult, ALU.add,
                     wait=[(s_mm, vm_nk), (s_ac, s_ac.v)])
                p.drain("vector")
                p.release("ps4", s_dv)
                v_nk = s_dv.v

                a_op("activation", out=STAT[:, 16:32], in_=STAT[:, 0:16],
                     func=AF.Exp, wait=[(s_dv, v_nk)])
                v_eb = s_ac.v
                for lp in range(8):
                    bk = 6 + (lp % 2)
                    bank = PS[bk]
                    w0 = p.use(f"ps{bk}")
                    if lp == 0:
                        w0 = w0 + [(s_dv, v_nk), (s_dv, v_orf)]
                    for j in range(2):
                        l = 2 * lp + j
                        mm(bank[:, 256 * j:256 * j + MF],
                           SCB[hsl, 2048 + 128 * l:2176 + 128 * l],
                           ORFT[hsl, :], start=True, stop=True,
                           wait=(w0 if j == 0 else ()))
                    vm = s_mm.v
                    a_op("activation",
                         out=SCA[:, eko + 512 * lp:eko + 512 * lp + 512],
                         in_=bank[:, 0:512], func=AF.Exp,
                         wait=[(s_mm, vm), (s_ac, v_eb)])
                    va = s_ac.v
                    for j in range(2):
                        l = 2 * lp + j
                        v_op("tensor_scalar",
                             SCA[:, eko + MF * l:eko + MF * l + MF],
                             SCA[:, eko + MF * l:eko + MF * l + MF],
                             STAT[:, 16 + l:17 + l], EPSP, ALU.mult, ALU.add,
                             wait=[(s_ac, va)])
                    p.release(f"ps{bk}", s_dv)
                p.drain("vector")
                v_ek = s_dv.v

                for mc2 in range(2):
                    bk = 4 + mc2
                    bank = PS[bk]
                    w0 = p.use(f"ps{bk}") + [(s_dv, v_ek), (s_dv, v_vdone)]
                    for l in range(NT):
                        mm(bank[:, 0:65],
                           SCA[:, eko + MF * l + 128 * mc2:
                               eko + MF * l + 128 * mc2 + 128],
                           SCA[:, 520 * l + 65 * hloc: 520 * l + 65 * hloc + 65],
                           start=(l == 0), stop=(l == NT - 1),
                           wait=(w0 if l == 0 else ()))
                    vm = s_mm.v
                    v_op("tensor_copy", SCA[:, kvo + 65 * mc2:kvo + 65 * mc2 + 65],
                         bank[:, 0:65], wait=[(s_mm, vm)])
                    p.release(f"ps{bk}", s_dv)
                p.drain("vector")
                v_kv = s_dv.v

                w2 = p.use("ps2")
                for mc2 in range(2):
                    mm(PS[2][0:1, 0:65], CONSTB[:, 128:129],
                       SCA[:, kvo + 65 * mc2:kvo + 65 * mc2 + 65],
                       start=(mc2 == 0), stop=(mc2 == 1),
                       wait=(w2 + [(s_dv, v_kv)] if mc2 == 0 else ()))
                vm_c = s_mm.v
                v_op("tensor_copy", SCA[0:1, 16642:16707], PS[2][0:1, 0:65],
                     wait=[(s_mm, vm_c)])
                p.drain("vector")
                p.release("ps2", s_dv)
                v_c = s_dv.v

                v_op("tensor_mul", SCB[:, 4096:6144], SCB[:, 0:2048], SCB[:, 0:2048],
                     wait=[(s_mm, s_mm.v)])
                p.drain("vector")
                v_sq = s_dv.v
                for lb in range(4):
                    bk = 2 + (lb % 2)
                    w0 = p.use(f"ps{bk}") + [(s_dv, v_sq), (s_ac, s_ac.v)]
                    mm(PS[bk][0:1, :], onescol,
                       SCB[hsl, 4096 + 512 * lb:4608 + 512 * lb],
                       start=True, stop=True, wait=w0)
                    vm = s_mm.v
                    a_op("activation", out=SCF[0:1, 512 * lb:512 * lb + 512],
                         in_=PS[bk][0:1, :], func=AF.Exp, scale=S2A,
                         bias=CONST[0:1, 5:6], wait=[(s_mm, vm)])
                    p.release(f"ps{bk}", s_ac)
                v_g = s_ac.v

                for mc2 in range(2):
                    for lb in range(4):
                        bk = 6 + (lb % 2)
                        w0 = p.use(f"ps{bk}") + [(s_dv, v_sq)]
                        mm(PS[bk][:, :], ORFT[hsl, 128 * mc2:128 * mc2 + 128],
                           SCB[hsl, 512 * lb:512 * lb + 512], start=True, stop=True,
                           wait=w0)
                        vm = s_mm.v
                        a_op("activation",
                             out=SCA[:, eqo + 2048 * mc2 + 512 * lb:
                                     eqo + 2048 * mc2 + 512 * lb + 512],
                             in_=PS[bk][:, :], func=AF.Exp, wait=[(s_mm, vm)])
                        p.release(f"ps{bk}", s_ac)
                v_eq = s_ac.v

                for l in range(NT):
                    bk = l % 2
                    bank = PS[bk]
                    w0 = p.use(f"ps{bk}") + [(s_ac, v_eq), (s_dv, v_c), (s_ac, v_g)]
                    for mc2 in range(2):
                        mm(bank[:, 0:65],
                           SCA[:, eqo + 2048 * mc2 + 128 * l:
                               eqo + 2048 * mc2 + 128 * l + 128],
                           SCA[:, kvo + 65 * mc2:kvo + 65 * mc2 + 65],
                           start=(mc2 == 0), stop=False,
                           wait=(w0 if mc2 == 0 else ()))
                    mm(bank[:, 0:65], SCF[0:1, 128 * l:128 * l + 128],
                       SCA[0:1, 16642:16707], start=False, stop=True)
                    vm = s_mm.v
                    v_op("reciprocal", SM0[:, 0:1], bank[:, 64:65], wait=[(s_mm, vm)])
                    p.drain("vector")
                    v_op("tensor_scalar_mul",
                         SCB[:, 6144 + 128 * l + 64 * hh:6144 + 128 * l + 64 * hh + 64],
                         bank[:, 0:64], SM0[:, 0:1], wait=p.use("pairout"))
                    p.drain("vector")
                    p.release(f"ps{bk}", s_dv)
            vdp = s_dv.v
            for g_ in range(4):
                p.dma("sync",
                      APx(d_ya, 128 * pp_i + g_ * 4 * 128 * D,
                          [[D, 128], [128 * D, 4], [1, 128]]),
                      APx(SCB, 6144 + g_ * 4 * 128, [[8192, 128], [128, 4], [1, 128]]),
                      s_pp[2], wait=([(s_dv, vdp)] if g_ == 0 else ()))
            p.release("pairout", s_pp[2])
        v_pairs = s_pp[2].v

        tv = []
        for c in range(ND):
            src = APx(d_ya, c * 128, [[D, S], [1, 128]])
            w = [(s_pp[2], v_pairs), (s_mm, s_mm.v), (s_dv, s_dv.v)] if c == 0 else ()
            tv.append(p.dma("sync", SCA[:, c * S:(c + 1) * S], src, s_tp, wait=w,
                            transpose=True))
        v_ya = tv[-1]
        for l in range(NT):
            for hf in range(2):
                bk = 2 * (l % 2) + hf
                bank = PS[bk]
                w0 = p.use(f"ps{bk}") + [(s_tp, v_ya), (s_wl[2], v_wo),
                                         (s_gld, v_crow)]
                for c in range(ND):
                    mm(bank[:, :], SCA[:, c * S + 128 * l: c * S + 128 * l + 128],
                       WB2[:, c * D + 512 * hf: c * D + 512 * hf + 512],
                       start=(c == 0), stop=False, wait=(w0 if c == 0 else ()))
                mm(bank[:, :], CONSTB[0:1, 0:128],
                   CROW[0:1, 1024 + 512 * hf:1536 + 512 * hf], start=False, stop=True)
                vm = s_mm.v
                dst = R[:, l * D + 512 * hf: l * D + 512 * hf + 512]
                v_op("tensor_tensor", dst, bank[:, :], dst, ALU.add,
                     wait=[(s_mm, vm)])
                if hf == 1:
                    p.release(f"rt{l}", s_dv)
                p.release(f"ps{bk}", s_dv)
        p.release("wb2", s_mm)
        p.drain("vector")

    def phase_mla(v_xnt, v_gain):
        vmm0 = s_mm.v

        def vcol(l):
            return 520 * l if l < 15 else 16512

        cast_load(WB1[:, 0:4096], wropek, D, 512, s_wl[1], wait=p.use("wb1"))
        cast_load(WB1[:, 4096:6144], wkvd, D, LAT, s_wl[1])
        vl_wb1 = cast_load(WB1[:, 6144:8192], wqd, D, LAT, s_wl[1])
        cast_load(WB2[:, 0:2048], wku_p, LAT, 1024, s_wl[2], wait=p.use("wb2"))
        cast_load(WB2[:, 2048:4096], wq_c, LAT, 1024, s_wl[2])
        vl_wb2 = cast_load(WB2[:, 4096:6144], wvu, LAT, D, s_wl[2])
        vt0 = p.dma("gpsimd", WB0[:, 0:2048], APx(ropec, 0, [[S, 128], [1, S]]),
                    s_tb, wait=p.use("wb0"))
        vt1 = p.dma("gpsimd", WB0[:, 2048:4096], APx(ropes, 0, [[S, 128], [1, S]]),
                    s_tb, wait=[(s_tb, vt0)])
        vt2 = p.dma("gpsimd", WB0[:, 4096:6144], APx(ropec2, 0, [[S, 128], [1, S]]),
                    s_tb, wait=[(s_tb, vt1)])
        vl_tbl = p.dma("gpsimd", WB0[:, 6144:8192],
                       APx(ropes2, 0, [[S, 128], [1, S]]), s_tb,
                       wait=[(s_tb, vt2)])

        for (wb, woff, swl, wv_, dsto) in ((WB1, 4096, s_wl[1], vl_wb1, 0),
                                           (WB1, 6144, s_wl[1], vl_wb1, 4096)):
            for ic in range(2):
                for lb in range(4):
                    bk = lb % 2
                    bank = PS[bk]
                    w0 = p.use(f"ps{bk}") + [(s_dv, v_xnt), (s_dv, v_gain), (swl, wv_)]
                    for c in range(ND):
                        mm(bank[:, :],
                           wb[:, woff + c * LAT + 128 * ic:
                              woff + c * LAT + 128 * ic + 128],
                           XNT[:, c * S + 512 * lb: c * S + 512 * lb + 512],
                           start=(c == 0), stop=(c == ND - 1),
                           wait=(w0 if c == 0 else ()))
                    vm = s_mm.v
                    v_op("tensor_copy",
                         SCB[:, dsto + 2048 * ic + 512 * lb:
                             dsto + 2048 * ic + 512 * lb + 512],
                         bank[:, :], wait=[(s_mm, vm)])
                    p.release(f"ps{bk}", s_dv)
        v_lat = s_dv.v
        if stop_after == "mla_lat":
            dump_dbg()
            return True

        for h2 in range(2):
            for c2 in range(2):
                hc = 2 * h2 + c2
                kro = 8320 + 4096 * h2 + 2048 * c2
                for lb in range(4):
                    bk = 2 + (lb % 2)
                    bank = PS[bk]
                    w0 = p.use(f"ps{bk}") + [(s_dv, v_xnt), (s_dv, v_gain),
                                             (s_wl[1], vl_wb1)]
                    for c in range(ND):
                        mm(bank[:, :],
                           WB1[:, c * 512 + 128 * hc: c * 512 + 128 * hc + 128],
                           XNT[:, c * S + 512 * lb: c * S + 512 * lb + 512],
                           start=(c == 0), stop=(c == ND - 1),
                           wait=(w0 if c == 0 else ()))
                    vm = s_mm.v
                    v_op("tensor_scalar_mul",
                         SCA[:, kro + 512 * lb: kro + 512 * lb + 512],
                         bank[:, :], SQ8, wait=[(s_mm, vm)])
                    p.release(f"ps{bk}", s_dv)
        p.drain("vector")
        v_raws = s_dv.v
        if stop_after == "mla_raws":
            dump_dbg()
            return True

        p.release("wb1", s_mm)
        mask32 = [(i + 16) % 32 for i in range(32)]
        vl_wo = None
        v_yq = {}
        vmm_latdone = 0
        for h2 in range(2):
            v_scpre = s_sc.v
            for pr in range(4):
                gp = 4 * h2 + pr
                for lb in range(4):
                    bk = lb % 2
                    bank = PS[bk]
                    w0 = p.use(f"ps{bk}") + [(s_dv, v_lat), (s_wl[2], vl_wb2)]
                    for ic in range(2):
                        mm(bank[:, :],
                           WB2[:, ic * 1024 + 128 * gp: ic * 1024 + 128 * gp + 128],
                           SCB[:, 2048 * ic + 512 * lb: 2048 * ic + 512 * lb + 512],
                           start=(ic == 0), stop=(ic == 1),
                           wait=(w0 if ic == 0 else ()))
                    vm = s_mm.v
                    v_op("tensor_scalar_mul",
                         XNT[:, 2048 * pr + 512 * lb: 2048 * pr + 512 * lb + 512],
                         bank[:, :], SQ8,
                         wait=[(s_mm, vm)] + ([(s_sc, v_scpre)]
                                              if pr == 0 and lb == 0 else []))
                    p.release(f"ps{bk}", s_dv)
            p.drain("vector")
            v_knope = s_dv.v

            v_scpre2 = s_sc.v
            for c2 in range(2):
                kro = 8320 + 4096 * h2 + 2048 * c2
                raw = SCA[:, kro: kro + 2048]
                shuf = SCF[:, 0:2048]
                w0 = [(s_dv, v_raws), (s_tb, vl_tbl), (s_mm, s_mm.v),
                      (s_ac, s_ac.v)]
                w0 += (p.use("slab") + p.use("pt0") + p.use("pt1")
                       + p.use("pt2") + p.use("stg") + [(s_pp[3], s_pp[3].v)])
                v_op("stream_shuffle", shuf, raw, mask=mask32, wait=w0)
                p.drain("vector")
                v_op("tensor_mul", shuf, shuf, WB0[:, 2048:4096])
                p.drain("vector")
                v_op("tensor_mul", raw, raw, WB0[:, 0:2048])
                p.drain("vector")
                v_op("tensor_add", raw, raw, shuf)
                p.drain("vector")
                p.release("slab", s_dv)
                v_kap = s_dv.v
                for j in range(4):
                    hlk = 4 * c2 + j
                    prk = hlk // 2
                    ro = 32 + 64 * (hlk % 2)
                    wsc = ([(s_dv, v_kap), (s_dv, v_knope),
                            (s_sc, v_scpre2)] if j == 0 else ())
                    p.dma("sync", XNT[ro:ro + 32, 2048 * prk: 2048 * prk + 2048],
                          SCA[32 * j:32 * j + 32, kro: kro + 2048], s_sc, wait=wsc)
            v_scat_all = s_sc.v

            if h2 == 0:
                wkz = ([(s_sc, v_scat_all), (s_mm, s_mm.v)]
                       + p.use("kst0") + p.use("kst1"))
                v_op("memset", SCA[64:128, 8320:10368], 0.0, wait=wkz)
                v_op("memset", SCA[0:64, 10368:12416], 0.0)
                p.drain("vector")
                v_kz = s_dv.v

            for pr in range(4):
                gp = 4 * h2 + pr
                for lb in range(4):
                    bk = 2 + (lb % 2)
                    bank = PS[bk]
                    w0 = p.use(f"ps{bk}") + [(s_dv, v_lat), (s_wl[2], vl_wb2)]
                    for ic in range(2):
                        mm(bank[:, :],
                           WB2[:, 2048 + ic * 1024 + 128 * gp:
                               2048 + ic * 1024 + 128 * gp + 128],
                           SCB[:, 4096 + 2048 * ic + 512 * lb:
                               4096 + 2048 * ic + 512 * lb + 512],
                           start=(ic == 0), stop=(ic == 1),
                           wait=(w0 if ic == 0 else ()))
                    vm = s_mm.v
                    v_op("tensor_scalar_mul",
                         XNT[:, 8192 + 2048 * pr + 512 * lb:
                             8192 + 2048 * pr + 512 * lb + 512],
                         bank[:, :], SQ8, wait=[(s_mm, vm)])
                    p.release(f"ps{bk}", s_dv)
                p.drain("vector")
                qsl = XNT[:, 8192 + 2048 * pr: 8192 + 2048 * pr + 2048]
                shuf = SCF[:, 0:2048]
                w0 = ([(s_tb, vl_tbl)] + p.use("slab") + p.use("pt0")
                      + p.use("pt1") + p.use("pt2") + p.use("stg")
                      + [(s_pp[3], s_pp[3].v)])
                v_op("stream_shuffle", shuf, qsl, mask=mask32, wait=w0)
                p.drain("vector")
                v_op("tensor_mul", shuf, shuf, WB0[:, 6144:8192])
                p.drain("vector")
                v_op("tensor_mul", qsl, qsl, WB0[:, 4096:6144])
                p.drain("vector")
                v_op("tensor_add", qsl, qsl, shuf)
                p.drain("vector")
                p.release("slab", s_dv)
            v_kkqq = s_dv.v

            if h2 == 1 and vl_wo is None:
                vl_wo = cast_load(WB0[:, :], wo_m, D, D, s_wl[0], wait=[(s_dv, v_kkqq)])

            v_op("memset", APx(SCA, 64, [[17408, 128], [520, 15], [65, 8], [1, 1]]),
                 1.0, wait=[(s_dv, v_raws), (s_mm, s_mm.v)])
            v_op("memset", APx(SCA, 16512 + 64, [[17408, 128], [65, 8], [1, 1]]), 1.0)
            v_ones2 = s_dv.v
            for l in range(NT):
                bk = 4 + (l % 2)
                bank = PS[bk]
                w0 = p.use(f"ps{bk}") + [(s_dv, v_lat), (s_wl[2], vl_wb2),
                                         (s_dv, v_ones2)]
                for ic in range(2):
                    mm(bank[:, :],
                       SCB[:, 2048 * ic + 128 * l: 2048 * ic + 128 * l + 128],
                       WB2[:, 4096 + 1024 * ic + 512 * h2:
                           4096 + 1024 * ic + 512 * h2 + 512],
                       start=(ic == 0), stop=(ic == 1), wait=(w0 if ic == 0 else ()))
                vm = s_mm.v
                dst = APx(SCA, vcol(l), [[17408, 128], [65, 8], [1, 64]])
                src = APx(PS[bk], 0, [[512, 128], [64, 8], [1, 64]])
                v_op("tensor_copy", dst, src, wait=[(s_mm, vm)])
                p.release(f"ps{bk}", s_dv)
            if h2 == 1:
                p.release("wb2", s_mm)
                vmm_latdone = s_mm.v
            p.drain("vector")
            v_vh = s_dv.v
            if stop_after == f"mla_v{h2}":
                dump_dbg()
                return True

            v_stg_pair = 0
            pair_vmh = 0
            for hl in range(8):
                gh = 8 * h2 + hl
                pr = hl // 2
                par = hl % 2
                if par == 0:
                    for pj in range(2):
                        w0 = (p.use(f"kst{pj}")
                              + [(s_dv, v_kz), (s_dv, v_knope), (s_dv, v_kkqq),
                                 (s_sc, v_scat_all)])
                        p.dma("sync",
                              SCA[64 * pj:64 * pj + 64,
                                  8320 + 2048 * pj: 8320 + 2048 * pj + 2048],
                              XNT[64 * pj:64 * pj + 64, 2048 * pr: 2048 * pr + 2048],
                              s_sc, wait=w0)
                    v_stg_pair = s_sc.v
                    kslab = XNT[:, 2048 * pr: 2048 * pr + 2048]
                    qslab = XNT[:, 8192 + 2048 * pr: 8192 + 2048 * pr + 2048]
                    kbank = (6, 7, 0, 1)
                    wsl = (p.use("slab") + p.use("pt0") + p.use("pt1")
                           + p.use("pt2") + p.use("stg")
                           + [(s_dv, v_vh), (s_sc, v_scat_all)])
                    v_op("tensor_mul", SCF[:, 0:2048], kslab, kslab, wait=wsl)
                    p.drain("vector")
                    v_sqk = s_dv.v
                    for kb in range(4):
                        bk = kbank[kb]
                        mm(PS[bk][:, :], CONSTB[:, 130:258],
                           SCF[:, 512 * kb: 512 * kb + 512], start=True, stop=True,
                           wait=p.use(f"ps{bk}") + [(s_dv, v_sqk)])
                    vmm_k = s_mm.v
                    for kb in range(4):
                        v_op("reduce_max", SM0[:, kb:kb + 1], PS[kbank[kb]][:, :],
                             axis=AX, wait=([(s_mm, vmm_k)] if kb == 0 else ()))
                    p.drain("vector")
                    v_op("reduce_max", SM0[:, 4:5], SM0[:, 0:4], axis=AX)
                    p.drain("vector")
                    v_mk = s_dv.v
                    for kb in range(4):
                        p.release(f"ps{kbank[kb]}", s_dv)
                    v_op("tensor_mul", SCF[:, 0:2048], qslab, qslab,
                         wait=[(s_mm, vmm_k)])
                    p.drain("vector")
                    v_sqq = s_dv.v
                    for qb in range(4):
                        bk = kbank[qb]
                        mm(PS[bk][:, :], CONSTB[:, 130:258],
                           SCF[:, 512 * qb: 512 * qb + 512], start=True, stop=True,
                           wait=p.use(f"ps{bk}") + [(s_dv, v_sqq)])
                    p.release("slab", s_mm)
                    vmm_q = s_mm.v
                    for qb in range(4):
                        v_op("reduce_max", SM0[:, qb:qb + 1], PS[kbank[qb]][:, :],
                             axis=AX, wait=([(s_mm, vmm_q), (s_dv, v_mk)]
                                            if qb == 0 else ()))
                    p.drain("vector")
                    v_op("tensor_scalar", SCF[:, 2048:2052], SM0[:, 0:4],
                         SM0[:, 4:5], None, ALU.add, wait=p.use("mhtot"))
                    p.drain("vector")
                    v_tot = s_dv.v
                    for qb in range(4):
                        p.release(f"ps{kbank[qb]}", s_dv)
                    w6 = p.use("ps6")
                    mm(PS[6][:, 0:4], CONSTB[:, 258:386], SCF[:, 2048:2052],
                       start=True, stop=True, wait=w6 + [(s_dv, v_tot)])
                    mm(PS[6][:, 4:8], CONSTB[:, 386:514], SCF[:, 2048:2052],
                       start=True, stop=True)
                    p.release("mhtot", s_mm)
                    vm_mh = s_mm.v
                    v_op("tensor_copy", SM0[:, 8:16], PS[6][:, 0:8],
                         wait=[(s_mm, vm_mh)] + p.use("mhcols"))
                    p.drain("vector")
                    p.release("ps6", s_dv)
                    pair_vmh = s_dv.v
                v_mh = pair_vmh
                v_stg = v_stg_pair
                ksl = SCA[:, 8320 + 2048 * par: 8320 + 2048 * par + 2048]

                for qb in range(4):
                    biascol = SM0[:, 8 + 4 * par + qb: 9 + 4 * par + qb]
                    w5 = p.use("ps5")
                    nkc = 4 * (qb + 1)
                    prev = None
                    for kc in range(nkc):
                        bk = (2, 3, 6)[kc % 3]
                        sbank = PS[bk]
                        w0 = p.use(f"ps{bk}") + [(s_dv, v_mh), (s_sc, v_stg)]
                        mm(sbank[:, :], ksl[:, 128 * kc: 128 * kc + 128],
                           XNT[:, 8192 + 2048 * pr + 512 * qb:
                               8192 + 2048 * pr + 512 * qb + 512],
                           start=True, stop=True, wait=w0)
                        vm = s_mm.v
                        if prev is not None:
                            pkc, pslot, v_pt = prev
                            mm(PS[5][:, :],
                               SCA[:, vcol(pkc) + 65 * hl:vcol(pkc) + 65 * hl + 128],
                               SCF[:, 512 * pslot:512 * pslot + 512],
                               start=(pkc == 0), stop=False,
                               wait=[v_pt, (s_dv, v_vh)] + (w5 if pkc == 0 else []))
                            p.release(f"pt{pslot}", s_mm)
                        pt = SCF[:, 512 * (kc % 3):512 * (kc % 3) + 512]
                        ptres = f"pt{kc % 3}"
                        wpt = p.use(ptres)
                        jst = 128 * kc - 512 * qb
                        if jst < 0:
                            a_op("activation", out=pt, in_=sbank[:, :], func=AF.Exp,
                                 bias=biascol, wait=[(s_mm, vm), (s_dv, v_mh)] + wpt)
                            v_pt = (s_ac, s_ac.v)
                        else:
                            if jst > 0:
                                v_op("memset", pt[:, 0:jst], 0.0, wait=wpt)
                            a_op("activation", out=pt[:, jst:512],
                                 in_=sbank[:, jst:512], func=AF.Exp, bias=biascol,
                                 wait=[(s_mm, vm), (s_dv, v_mh)] + (wpt if jst == 0 else []))
                            va = s_ac.v
                            v_op("tensor_mul", pt[:, jst:jst + 128],
                                 pt[:, jst:jst + 128], CONSTB[:, 0:128],
                                 wait=[(s_ac, va)])
                            p.drain("vector")
                            v_pt = (s_dv, s_dv.v)
                        p.release(f"ps{bk}", s_ac if jst < 0 else s_dv)
                        prev = (kc, kc % 3, v_pt)
                    pkc, pslot, v_pt = prev
                    mm(PS[5][:, :],
                       SCA[:, vcol(pkc) + 65 * hl:vcol(pkc) + 65 * hl + 128],
                       SCF[:, 512 * pslot:512 * pslot + 512],
                       start=(pkc == 0), stop=True,
                       wait=[v_pt, (s_dv, v_vh)] + (w5 if pkc == 0 else []))
                    p.release(f"pt{pslot}", s_mm)
                    if qb == 3:
                        p.release("mhcols", s_ac)
                        p.release(f"kst{par}", s_mm)
                    vm_pv = s_mm.v
                    wst = p.use("stg")
                    v_op("tensor_copy", SCF[0:65, 1536:2048], PS[5][0:65, :],
                         wait=[(s_mm, vm_pv)] + wst)
                    p.drain("vector")
                    p.release("ps5", s_dv)
                    vd = s_dv.v
                    p.dma("gpsimd", APx(d_yraw, 64 * gh * S + 512 * qb,
                                        [[S, 64], [1, 512]]),
                          SCF[0:64, 1536:2048], s_yd, wait=[(s_dv, vd)])
                    p.dma("gpsimd", APx(d_yraw, (1024 + gh) * S + 512 * qb,
                                        [[S, 1], [1, 512]]),
                          SCF[64:65, 1536:2048], s_yd)
                    p.release("stg", s_yd)
                    if h2 == 1 and hl == 7:
                        v_yq[qb] = s_yd.v
        v_yraw = s_yd.v
        if stop_after == "mla_flash":
            p.wait("sync", (s_yd, v_yraw))
            dump_dbg()
            return True

        v_y2h1 = None
        for l in range(NT):
            slot = SCB[:, (l % 4) * 1040:(l % 4) * 1040 + 1040]
            w = [(s_yd, v_yq[l // 4]), (s_mm, vmm_latdone)]
            w = list(w) + p.use(f"ytk{l % 4}")
            vt = p.dma("sync", slot, APx(d_yraw, 128 * l, [[S, 1040], [1, 128]]),
                       s_pp[l % 4], wait=w, transpose=True)
            v_op("reciprocal", STAT[:, 112:128], slot[:, 1024:1040],
                 wait=[(s_pp[l % 4], vt)])
            p.drain("vector")
            for h_ in range(H):
                v_op("tensor_scalar_mul", slot[:, 64 * h_:64 * h_ + 64],
                     slot[:, 64 * h_:64 * h_ + 64], STAT[:, 112 + h_:113 + h_])
            p.drain("vector")
            vd = s_dv.v
            p.dma("gpsimd", APx(d_y2, 128 * l * D, [[D, 128], [1, D]]),
                  slot[:, 0:1024], s_rl[l % 4], wait=[(s_dv, vd)])
            p.release(f"ytk{l % 4}", s_rl[l % 4])
            if l == 7:
                v_y2h1 = [(s_rl[i], s_rl[i].v) for i in range(4)]
        v_y2 = [(s_rl[i], s_rl[i].v) for i in range(4)]
        if stop_after == "mla_trip":
            dump_dbg()
            return True

        v_y2t_h1 = 0
        for half in range(2):
            for c in range(ND):
                src = APx(d_y2, half * 1024 * D + c * 128, [[D, 1024], [1, 128]])
                w = [(s_tp, s_tp.v)]
                if c == 0 and half == 0:
                    w += v_y2h1 + [(s_mm, vmm_latdone), (s_dv, s_dv.v)]
                elif c == 0:
                    w += v_y2
                p.dma("sync",
                      SCA[:, c * S + 1024 * half:c * S + 1024 * half + 1024],
                      src, s_tp, wait=w, transpose=True)
            if half == 0:
                v_y2t_h1 = s_tp.v
        v_y2t = s_tp.v
        for l in range(NT):
            for hf in range(2):
                bk = 2 * (l % 2) + hf
                bank = PS[bk]
                w0 = p.use(f"ps{bk}") + [(s_tp, v_y2t_h1 if l < 8 else v_y2t),
                                         (s_wl[0], vl_wo)]
                for c in range(ND):
                    mm(bank[:, :], SCA[:, c * S + 128 * l: c * S + 128 * l + 128],
                       WB0[:, c * D + 512 * hf: c * D + 512 * hf + 512],
                       start=(c == 0), stop=(c == ND - 1), wait=(w0 if c == 0 else ()))
                vm = s_mm.v
                dst = R[:, l * D + 512 * hf: l * D + 512 * hf + 512]
                v_op("tensor_tensor", dst, bank[:, :], dst, ALU.add, wait=[(s_mm, vm)])
                if hf == 1:
                    p.release(f"rt{l}", s_dv)
                p.release(f"ps{bk}", s_dv)
        p.release("wb0", s_mm)
        p.drain("vector")

    def phase_moe(v_xnt, v_gain, v_lodump):
        vmm0 = s_mm.v
        v_op("memset", APx(PR, 7, [[128, 128], [8, 16], [1, 1]]), -1e30,
             wait=[(s_ac, s_ac.v)])
        v_prpad = s_dv.v
        def rchunk_dma(c):
            hslot = SCB[:, (c % 2) * 4096:(c % 2) * 4096 + 2048]
            lslot = SCB[:, (c % 2) * 4096 + 2048:(c % 2) * 4096 + 4096]
            w = p.use(f"rch{c % 2}")
            if c == 0:
                w = w + [(s_pp[0], v_lodump[0]), (s_pp[1], v_lodump[1]),
                         (s_pp[2], v_lodump[2]), (s_pp[3], v_lodump[3]),
                         (s_mm, vmm0), (s_dv, s_dv.v)]
            sem = s_pp[2 + c % 2]
            p.dma("sync", hslot, APx(d_xn, c * 128, [[D, S], [1, 128]]),
                  sem, wait=w, transpose=True)
            vl = p.dma("sync", lslot, APx(d_lo, c * 128, [[D, S], [1, 128]]),
                  sem, transpose=True)
            return (hslot, lslot, (sem, vl))

        rslots = {0: rchunk_dma(0), 1: rchunk_dma(1)}
        vl_g = cast_load(WB0[:, :], wg_s, D, D, s_wl[0], wait=p.use("wb0"))
        vl_u = cast_load(WB1[:, :], wu_s, D, D, s_wl[1], wait=p.use("wb1"))
        vl_d = cast_load(WB2[:, :], wd_s, D, D, s_wl[2], wait=p.use("wb2"))
        for ei in range(ER + 1):
            e = ei - 1
            if ei == 1:
                for c in range(ND):
                    hslot, lslot, vl = rslots[c]
                    for b in range(4):
                        bk = 4 + b
                        w0 = [vl] if b == 0 else []
                        if c == 0:
                            w0 = w0 + p.use(f"ps{bk}")
                        mm(PS[bk][0:7, :], WROUT[:, ER * c:ER * c + ER],
                           hslot[:, 512 * b:512 * b + 512],
                           start=(c == 0), stop=False, wait=w0)
                        mm(PS[bk][0:7, :], WRL[:, ER * c:ER * c + ER],
                           hslot[:, 512 * b:512 * b + 512], start=False, stop=False)
                        mm(PS[bk][0:7, :], WROUT[:, ER * c:ER * c + ER],
                           lslot[:, 512 * b:512 * b + 512], start=False,
                           stop=(c == ND - 1))
                    p.release(f"rch{c % 2}", s_mm)
                    if c + 2 < ND:
                        rslots[c + 2] = rchunk_dma(c + 2)
                vmm_rt = s_mm.v
                for b in range(4):
                    bk = 4 + b
                    for j in range(4):
                        l = 4 * b + j
                        v_op("tensor_copy", RLG[0:7, :], PS[bk][0:7, 128 * j:128 * j + 128],
                             wait=[(s_mm, vmm_rt)] + p.use("rlg"))
                        p.drain("vector")
                        vr = s_dv.v
                        w7 = p.use("ps3")

                        def tb(en):
                            return en.transpose(PS[3][:, 0:7], RLG[0:7, :], IDENT[0:7, 0:7])
                        p.on("tensor", tb, wait=w7 + [(s_dv, vr)], inc=s_mm, n=1)
                        p.release("rlg", s_mm)
                        vm = s_mm.v
                        v_op("tensor_tensor", SM0[:, 0:ER], PS[3][:, 0:ER], RBR[:, :], ALU.add,
                             wait=[(s_mm, vm), (s_ac, s_ac.v)])
                        p.drain("vector")
                        p.release("ps3", s_dv)
                        v_op("tensor_copy", PR[:, 8 * l:8 * l + ER], SM0[:, 0:ER])
                        p.drain("vector")
                    p.release(f"ps{bk}", s_dv)
                v_probs = s_dv.v

                for l in range(NT):
                    blk = PR[:, 8 * l:8 * l + 8]
                    w0 = [(s_dv, v_probs), (s_dv, v_prpad)] if l == 0 else []
                    v_op("max", out=SM0[:, 0:8], in_=blk, wait=w0)
                    p.drain("vector")
                    va0 = s_ac.v
                    a_op("activation", out=SM0[:, 8:10], in_=SM0[:, 0:2], func=AF.Sigmoid,
                         wait=[(s_dv, s_dv.v)])
                    va = s_ac.v
                    v_op("tensor_tensor", SM0[:, 10:11], SM0[:, 8:9], SM0[:, 9:10], ALU.add,
                         wait=[(s_ac, va)])
                    p.drain("vector")
                    v_op("reciprocal", SM0[:, 11:12], SM0[:, 10:11])
                    p.drain("vector")
                    v_op("tensor_mul", SM0[:, 12:13], SM0[:, 8:9], SM0[:, 11:12])
                    v_op("tensor_mul", SM0[:, 13:14], SM0[:, 9:10], SM0[:, 11:12])
                    p.drain("vector")
                    v_op("tensor_scalar", WTS[:, 8 * l:8 * l + ER], PR[:, 8 * l:8 * l + ER],
                         SM0[:, 0:1], SM0[:, 12:13], ALU.is_equal, ALU.mult)
                    p.drain("vector")
                    v_op("tensor_scalar", STAT[:, 128:128 + ER], PR[:, 8 * l:8 * l + ER],
                         SM0[:, 1:2], SM0[:, 13:14], ALU.is_equal, ALU.mult)
                    p.drain("vector")
                    v_op("tensor_tensor", WTS[:, 8 * l:8 * l + ER], WTS[:, 8 * l:8 * l + ER],
                         STAT[:, 128:128 + ER], ALU.add)
                    p.drain("vector")
                v_wts = s_dv.v

            def emit_up_evacs(pic, plh, pbks, pvm):
                for lb2 in range(2):
                    lb = 2 * plh + lb2
                    a_op("activation", out=SCA[:, 16384:16896],
                         in_=PS[pbks[lb2]][:, :], func=AF.Silu,
                         wait=[(s_mm, pvm)] + p.use("silu_s"))
                    va_si = s_ac.v
                    p.release(f"ps{pbks[lb2]}", s_ac)
                    v_op("tensor_tensor",
                         SCA[:, 2048 * pic + 512 * lb:2048 * pic + 512 * lb + 512],
                         SCA[:, 16384:16896], PS[pbks[2 + lb2]][:, :], ALU.mult,
                         wait=[(s_mm, pvm), (s_ac, va_si)] + p.use("silu_s"))
                    p.drain("vector")
                    p.release("silu_s", s_dv)
                    p.release(f"ps{pbks[2 + lb2]}", s_dv)

            pend = None
            for ic in range(ND):
                for lh in range(2):
                    t = 2 * ic + lh
                    bks = (0, 1, 2, 3) if t % 2 == 0 else (4, 5, 6, 7)
                    for (wslot, swl, vw, bsub) in ((WB0, s_wl[0], vl_g, (0, 1)),
                                                   (WB1, s_wl[1], vl_u, (2, 3))):
                        for lb2 in range(2):
                            bk = bks[bsub[lb2]]
                            bank = PS[bk]
                            lb = 2 * lh + lb2
                            w0 = p.use(f"ps{bk}") + [(s_dv, v_xnt), (s_dv, v_gain),
                                                     (swl, vw)]
                            for c in range(ND):
                                mm(bank[:, :],
                                   wslot[:, c * D + 128 * ic: c * D + 128 * ic + 128],
                                   XNT[:, c * S + 512 * lb: c * S + 512 * lb + 512],
                                   start=(c == 0), stop=(c == ND - 1),
                                   wait=(w0 if c == 0 else ()))
                    vm_up = s_mm.v
                    if pend is not None:
                        emit_up_evacs(*pend)
                    pend = (ic, lh, bks, vm_up)
            emit_up_evacs(*pend)
            v_hp = s_dv.v
            p.release("wb0", s_mm)
            p.release("wb1", s_mm)
            if ei < ER:
                vl_g = cast_load(WB0[:, :], wg_r, D, D, s_wl[0],
                                 wait=p.use("wb0"), row_off=ei * D)
                vl_u = cast_load(WB1[:, :], wu_r, D, D, s_wl[1],
                                 wait=p.use("wb1"), row_off=ei * D)
            for lc in range(NT):
                bks = (0, 1) if lc % 2 == 0 else (2, 3)
                for ic in range(ND):
                    for hf in range(2):
                        bk = bks[hf]
                        w0 = (p.use(f"ps{bk}") + [(s_dv, v_hp), (s_wl[2], vl_d)]
                              if ic == 0 else ())
                        mm(PS[bk][:, :],
                           SCA[:, 2048 * ic + 128 * lc: 2048 * ic + 128 * lc + 128],
                           WB2[:, ic * D + 512 * hf: ic * D + 512 * hf + 512],
                           start=(ic == 0), stop=(ic == ND - 1), wait=w0)
                vm = s_mm.v
                for hf in range(2):
                    bk = bks[hf]
                    dst = R[:, lc * D + 512 * hf: lc * D + 512 * hf + 512]
                    if e >= 0:
                        w0 = [(s_mm, vm)] + ([(s_dv, v_wts)]
                                             if ei == 1 and lc == 0 else [])
                        v_op("scalar_tensor_tensor", dst, PS[bk][:, :],
                             WTS[:, 8 * lc + e:8 * lc + e + 1], dst,
                             ALU.mult, ALU.add, wait=w0)
                    else:
                        v_op("tensor_tensor", dst, PS[bk][:, :], dst, ALU.add,
                             wait=[(s_mm, vm)])
                    if hf == 1 and ei == ER:
                        p.release(f"rt{lc}", s_dv)
                    p.release(f"ps{bk}", s_dv)
            p.release("wb2", s_mm)
            if ei < ER:
                vl_d = cast_load(WB2[:, :], wd_r, D, D, s_wl[2],
                                 wait=p.use("wb2"), row_off=ei * D)
        p.drain("vector")

    def phase_ffn(call):
        vst = norm_stats("ln", use_rt=True, fuse_rms=True)
        norm_apply_fused(vst, spill=True)
        v_gain = xnt_gain(WLN1)
        v_xnt = v_gain
        if stop_after is not None and stop_after == ("f_ln1" if call == 0 else "zz"):
            dump_dbg()
            return True
        if phase_mla(v_xnt, v_gain):
            return True
        if stop_after is not None and stop_after == ("mla1" if call == 0 else "zz"):
            dump_dbg()
            return True
        vst3 = norm_stats("rms", use_rt=True)
        ppv3 = norm_apply_dump("rms", vst3, dump=True, lo_dump=True)
        v_gain2 = xnt_gain(WLN2)
        phase_moe(v_gain2, v_gain2, ppv3)
        if stop_after is not None and stop_after == ("moe1" if call == 0 else "zz"):
            dump_dbg()
            return True
        for g_ in range(4):
            wg_ = [(s_mm, s_mm.v), (s_ac, s_ac.v)] if g_ == 0 else []
            for j_ in range(4):
                wg_ = wg_ + p.use(f"rt{4 * g_ + j_}")
            src = APx(sp_x, g_ * 4 * 128 * D, [[D, 128], [128 * D, 4], [1, D]])
            dst = R[:, g_ * 4 * D:(g_ + 1) * 4 * D]

            def bb(e, dst=dst, src=src):
                return e.dma_start(out=dst, in_=src, accum_op=ALU.add)
            vg = p.on("gpsimd", bb, wait=wg_, inc=s_rl[g_], n=16)
            for j_ in range(4):
                p.rel[f"rt{4 * g_ + j_}"] = (s_rl[g_], vg)
        if stop_after is not None and stop_after == ("ffn1" if call == 0 else "ffn2"):
            dump_dbg()
            return True
        return False

    def phase_attn_b():
        vst = norm_stats("ln", use_rt=True)
        v_xnt = norm_apply_dump("ln", vst)[4]
        vmm0 = s_mm.v
        vl_q = cast_load(WB0[:, :], wqb, D, D, s_wl[0], wait=p.use("wb0"))
        vl_k = cast_load(WB1[:, :], wkb, D, D, s_wl[1], wait=p.use("wb1"))
        vl_v = cast_load(WB2[:, :], wvb, D, D, s_wl[2], wait=p.use("wb2"))

        for c2 in range(ND):
            for lb in range(4):
                bk = lb % 2
                bank = PS[bk]
                w0 = p.use(f"ps{bk}") + [(s_dv, v_xnt), (s_wl[2], vl_v)]
                for c in range(ND):
                    mm(bank[:, :], WB2[:, c * D + 128 * c2: c * D + 128 * c2 + 128],
                       XNT[:, c * S + 512 * lb: c * S + 512 * lb + 512],
                       start=(c == 0), stop=(c == ND - 1), wait=(w0 if c == 0 else ()))
                vm = s_mm.v
                v_op("tensor_copy", SCA[:, 2048 * c2 + 512 * lb:2048 * c2 + 512 * lb + 512],
                     bank[:, :], wait=[(s_mm, vm)])
                p.release(f"ps{bk}", s_dv)
        p.release("wb2", s_mm)
        p.drain("vector")
        v_vt = s_dv.v

        for l in range(NT):
            for (wslot, swl, vw, bks, dsto) in ((WB0, s_wl[0], vl_q, (0, 1), 0),
                                                (WB1, s_wl[1], vl_k, (2, 3), 2048)):
                for hf in range(2):
                    bk = bks[hf]
                    bank = PS[bk]
                    w0 = p.use(f"ps{bk}") + [(s_dv, v_xnt), (swl, vw)]
                    for c in range(ND):
                        mm(bank[:, :],
                           XNT[:, c * S + 128 * l: c * S + 128 * l + 128],
                           wslot[:, c * D + 512 * hf: c * D + 512 * hf + 512],
                           start=(c == 0), stop=(c == ND - 1),
                           wait=(w0 if c == 0 else ()))
                    vm = s_mm.v
                    v_op("tensor_copy", SCB[:, dsto + 1024 * (l % 2) + 512 * hf:
                                            dsto + 1024 * (l % 2) + 512 * hf + 512],
                         bank[:, :], wait=[(s_mm, vm)])
                    p.release(f"ps{bk}", s_dv)
            p.drain("vector")
            v_qkt = s_dv.v
            w4 = p.use("ps4")
            for h_ in range(H):
                qsl = SCB[:, 1024 * (l % 2) + 64 * h_:1024 * (l % 2) + 64 * h_ + 64]
                ksl = SCB[:, 2048 + 1024 * (l % 2) + 64 * h_:
                          2048 + 1024 * (l % 2) + 64 * h_ + 64]
                po = 64 * (h_ % 2)
                mm(PS[4][po:po + 64, 64 * (h_ // 2):64 * (h_ // 2) + 64],
                   ksl, qsl, start=True, stop=True,
                   wait=(w4 + [(s_dv, v_qkt)] if h_ == 0 else ()))
            vm = s_mm.v
            v_op("tensor_copy", SCB[:, 4096:4608], PS[4][:, :], wait=[(s_mm, vm)])
            p.drain("vector")
            p.release("ps4", s_dv)
            w6 = p.use("ps6")
            if l == 0:
                v_op("tensor_copy", PS[6][:, :], SCB[:, 4096:4608], wait=w6)
            else:
                v_op("tensor_tensor", PS[6][:, :], SCB[:, 4096:4608], PS[6][:, :],
                     ALU.add, wait=w6)
            p.drain("vector")
            p.release("ps6", s_dv)
        v_z1 = s_dv.v
        p.release("wb0", s_mm)
        p.release("wb1", s_mm)
        vl_wo = cast_load(WB0[:, :], wob, D, D, s_wl[0], wait=p.use("wb0"))

        for c in range(8):
            zsl = PS[6][:, 64 * c:64 * c + 64]
            v_op("reduce_max", SM0[:, 0:1], zsl, axis=AX,
                 wait=([(s_dv, v_z1)] if c == 0 else ()))
            p.drain("vector")
            v_op("tensor_scalar_mul", SM0[:, 1:2], SM0[:, 0:1], -MLA_SC)
            p.drain("vector")
            vv = s_dv.v
            a_op("activation", out=SCB[:, 4608 + 64 * c:4672 + 64 * c],
                 in_=zsl, func=AF.Exp, scale=MLA_SC, bias=SM0[:, 1:2],
                 accum_out=SM0[:, 2:3], wait=[(s_dv, vv)])
            va = s_ac.v
            v_op("reciprocal", SM0[:, 3:4], SM0[:, 2:3], wait=[(s_ac, va)])
            p.drain("vector")
            v_op("tensor_scalar_mul", SCA[:, 2048 * c:2048 * c + 2048],
                 SCA[:, 2048 * c:2048 * c + 2048], SM0[:, 3:4])
            p.drain("vector")
        v_sm = s_dv.v

        vmm_z1done = s_mm.v
        for h_ in range(H):
            po = 64 * (h_ % 2)
            psl = slice(po, po + 64)
            for lb in range(4):
                bk = 2 + (lb % 2)
                bank = PS[bk]
                w0 = p.use(f"ps{bk}") + [(s_dv, v_sm)]
                mm(bank[po:po + 64, :],
                   SCB[psl, 4608 + 64 * (h_ // 2):4672 + 64 * (h_ // 2)],
                   SCA[psl, 2048 * (h_ // 2) + 512 * lb:2048 * (h_ // 2) + 512 * lb + 512],
                   start=True, stop=True, wait=w0)
                vm = s_mm.v
                v_op("tensor_copy",
                     XNT[psl, 2048 * (h_ // 2) + 512 * lb:2048 * (h_ // 2) + 512 * lb + 512],
                     bank[po:po + 64, :], wait=[(s_mm, vm)])
                p.release(f"ps{bk}", s_dv)
        p.drain("vector")
        v_z2 = s_dv.v

        for l in range(NT):
            for hf in range(2):
                bk = 4 + 2 * (l % 2) + hf
                bank = PS[bk]
                w0 = p.use(f"ps{bk}") + [(s_dv, v_z2), (s_wl[0], vl_wo)]
                for c in range(ND):
                    mm(bank[:, :], XNT[:, c * S + 128 * l: c * S + 128 * l + 128],
                       WB0[:, c * D + 512 * hf: c * D + 512 * hf + 512],
                       start=(c == 0), stop=(c == ND - 1), wait=(w0 if c == 0 else ()))
                vm = s_mm.v
                dst = R[:, l * D + 512 * hf: l * D + 512 * hf + 512]
                v_op("tensor_tensor", dst, bank[:, :], dst, ALU.add, wait=[(s_mm, vm)])
                if hf == 1:
                    p.release(f"rt{l}", s_dv)
                p.release(f"ps{bk}", s_dv)
        p.release("wb0", s_mm)
        p.drain("vector")

    v_wq0 = cast_load(WB0[:, :], wqa, D, D, s_wl[0])
    v_wk0 = cast_load(WB1[:, :], wka, D, D, s_wl[1])
    v_wv0 = cast_load(WB2[:, :], wva, D, D, s_wl[2])

    done = False
    phase_attn_a(v_wq0, v_wk0, v_wv0)
    if stop_after == "attn_a":
        dump_dbg()
        done = True
    if not done:
        done = phase_ffn(0)
    if not done and stop_after == "attn_b_pre":
        dump_dbg()
        done = True
    if not done:
        phase_attn_b()
        if stop_after == "attn_b":
            dump_dbg()
            done = True
    if not done:
        done = phase_ffn(1)

    wout = [(s_dv, s_dv.v), (s_ac, s_ac.v), (s_mm, s_mm.v), (s_gld, s_gld.v)]
    for i_ in range(NT):
        wout += p.use(f"rt{i_}")
    big_wdma("sync", out_t, R, s_st, wait=wout)
    p.wait("gpsimd", (s_st, s_st.v))
    p.finalize()
    return nc, di


def _rope_tables():
    inv = 1.0 / (10000.0 ** (np.arange(0, HH, 2, dtype=np.float64) / HH))
    t = np.arange(S, dtype=np.float64)
    freqs = np.einsum("i,j->ij", t, inv)
    emb = np.concatenate([freqs, freqs], -1)
    cos = np.cos(emb).T
    sin = np.sin(emb).T
    sinm = np.concatenate([-sin[0:16], sin[16:32]], 0)
    cos2 = np.tile(cos, (4, 1)).astype(np.float32)
    sinm2 = np.tile(sinm, (4, 1)).astype(np.float32)
    cos2[0:32] = 1.0
    cos2[64:96] = 1.0
    sinm2[0:32] = 0.0
    sinm2[64:96] = 0.0
    return (np.tile(cos, (4, 1)).astype(np.float32),
            np.tile(sinm, (4, 1)).astype(np.float32), cos2, sinm2)


def _host_inputs(inputs, core):
    import ml_dtypes
    z = inputs
    bf = ml_dtypes.bfloat16

    def T(a):
        return np.ascontiguousarray(np.asarray(a, dtype=np.float32).T)

    def C(a):
        return np.ascontiguousarray(np.asarray(a, dtype=np.float32))

    def fcols(b):
        return np.ascontiguousarray(np.asarray(b, np.float32).reshape(ND, 128).T)

    cos, sinm, cos2, sinm2 = _rope_tables()
    constf = np.zeros((128, 16), np.float32)
    constf[:, 0] = 1.0
    constf[:, 4] = LN_EPS
    constf[:, 5] = math.log(EPS_A)
    constb = np.zeros((128, 514), np.float32)
    pp_, jj = np.mgrid[0:128, 0:128]
    constb[:, 0:128] = (pp_ <= jj).astype(np.float32)
    constb[:, 128] = 1.0
    constb[:, 129] = -1.0
    constb[0:64, 130:194] = 1.0
    constb[64:128, 194:258] = 1.0
    constb[0:64, 258:386] = -1.0 / 128.0
    constb[64:128, 386:514] = -1.0 / 128.0
    wku_ = np.asarray(z["L0_ffn_W_ku"], np.float32)
    wqu_ = np.asarray(z["L0_ffn_W_qu"], np.float32)
    wrq_ = np.asarray(z["L0_ffn_W_ropeq"], np.float32)
    wku_p_ = np.zeros((LAT, 1024), np.float32)
    wq_c_ = np.zeros((LAT, 1024), np.float32)
    for h_ in range(H):
        base = 128 * (h_ // 2) + 64 * (h_ % 2)
        wku_p_[:, base:base + 32] = wku_[:, 32 * h_:32 * h_ + 32]
        wq_c_[:, base:base + 32] = wqu_[:, 32 * h_:32 * h_ + 32]
        wq_c_[:, base + 32:base + 64] = wrq_[:, 32 * h_:32 * h_ + 32]
    crow_ = np.zeros((1, 2048), np.float32)
    crow_[0, 0:1024] = np.asarray(z["L0_attn_a_bv"], np.float32)
    crow_[0, 1024:2048] = np.asarray(z["L0_attn_a_bo"], np.float32)
    return {
        "x": np.ascontiguousarray(np.asarray(z["L0_attn_a_x"])[core], dtype=np.float32),
        "wqa": T(z["L0_attn_a_Wq"]), "wka": T(z["L0_attn_a_Wk"]),
        "wva": T(z["L0_attn_a_Wv"]), "woa": T(z["L0_attn_a_Wo"]),
        "bqa": fcols(z["L0_attn_a_bq"]), "bka": fcols(z["L0_attn_a_bk"]),
        "orfT": np.ascontiguousarray(np.tile(np.asarray(z["L0_attn_a_orf"],
                                                        np.float32).T, (2, 1))),
        "wqb": T(z["L0_attn_b_Wq"]), "wkb": T(z["L0_attn_b_Wk"]),
        "wvb": T(z["L0_attn_b_Wv"]), "wob": T(z["L0_attn_b_Wo"]),
        "wkvd": C(z["L0_ffn_W_kvd"]), "wqd": C(z["L0_ffn_W_qd"]),
        "wku_p": wku_p_, "wq_c": wq_c_,
        "wvu": C(z["L0_ffn_W_vu"]),
        "wropek": C(z["L0_ffn_W_ropek"]),
        "wo_m": C(z["L0_ffn_W_o"]),
        "w_ln1f": fcols(z["L0_ffn_w_ln1"]), "w_ln2f": fcols(z["L0_ffn_w_ln2"]),
        "wg_s": C(z["L0_ffn_Wg_s"]), "wu_s": C(z["L0_ffn_Wu_s"]),
        "wd_s": C(z["L0_ffn_Wd_s"]),
        "wg_r": C(z["L0_ffn_Wg_r"]), "wu_r": C(z["L0_ffn_Wu_r"]),
        "wd_r": C(z["L0_ffn_Wd_r"]),
        "wrout": C(z["L0_ffn_W_router"]),
        "rb_row": np.ascontiguousarray(np.broadcast_to(
            np.asarray(z["L0_ffn_routing_bias"], np.float32), (128, ER))),
        "identf": np.eye(128, dtype=np.float32),
        "identb": np.eye(128, dtype=np.float32).astype(bf),
        "constf": constf,
        "constb": constb.astype(bf),
        "crow": crow_,
        "ropec": cos,
        "ropes": sinm,
        "ropec2": cos2,
        "ropes2": sinm2,
    }


def kernel(**inputs):
    nc, _ = build_nc()
    in_maps = [_host_inputs(inputs, core) for core in range(8)]
    res = run_bass_kernel_spmd(nc, in_maps, core_ids=list(range(8)))
    return np.stack([np.asarray(r["out"], np.float32) for r in res.results], axis=0)
